# version 13
# baseline (speedup 1.0000x reference)
import numpy as np
import ml_dtypes
import concourse.bass as bass
import concourse.mybir as mybir
from concourse.bass_utils import run_bass_kernel_spmd

F32 = mybir.dt.float32
BF16 = mybir.dt.bfloat16
AF = mybir.ActivationFunctionType
ALU = mybir.AluOpType
AX = mybir.AxisListType

P = 128
TOK = 2048
NT = TOK // P
E = 1024
EC = E // P
H = 16
D = 64
FF = 4096
FC = FF // P
AG = 4
BG = 2
NAG = NT // AG
NBG = NT // BG

W1_OFF = 0
W2_OFF = 32768
WQ_OFF = 40960
WREG_COLS = 65536
VEXW = H * (D + 1)


class Sched:

    NROT = 6
    ENGQ = {"sync": "LX", "gpsimd": "LW", "scalar": "SP", "vector": "LV"}
    ESEM = {"tensor": "T", "vector": "V", "scalar": "S", "gpsimd": "G"}

    def __init__(self):
        self.ops = []
        self.cnt = {}
        self.wp = {}
        self.qidx = {}

    def _bump(self, sem, amt):
        self.cnt[sem] = self.cnt.get(sem, 0) + amt
        return self.cnt[sem]

    def op(self, engine, fn, waits=(), wp=None, dma=False, cc=False, nodrain=False):
        wl = []
        for w in waits:
            if w is None:
                continue
            assert w in self.wp, f"waypoint {w!r} not defined yet"
            wl.append(self.wp[w])
        inc = None
        if dma:
            q = self.ENGQ[engine]
            i = self.qidx.get(q, 0)
            self.qidx[q] = i + 1
            sem = f"{q}{i % self.NROT}"
            prev = self.cnt.get(sem, 0)
            if prev:
                wl.append((sem, prev))
            v = self._bump(sem, 16)
            inc = (sem, 16)
            if wp:
                self.wp[wp] = (sem, v)
        elif cc:
            v = self._bump("CC", 1)
            inc = ("CC", 1)
            if wp:
                self.wp[wp] = ("CC", v)
        elif wp:
            sem = self.ESEM[engine]
            v = self._bump(sem, 1)
            inc = (sem, 1)
            self.wp[wp] = (sem, v)
        drain = (not dma) and (not cc) and (not nodrain) and engine in ("vector", "scalar")
        self.ops.append((engine, fn, wl, inc, drain))

    def emit(self, eng_name, eng, sems):
        last = {}
        for engine, fn, waits, inc, drain in self.ops:
            if engine != eng_name:
                continue
            for sem, val in waits:
                if last.get(sem, 0) >= val:
                    continue
                eng.wait_ge(sems[sem], val)
                last[sem] = val
            if drain:
                eng.drain()
            instr = fn(eng)
            if inc is not None:
                assert instr is not None
                instr.then_inc(sems[inc[0]], inc[1])


def build_nc():
    nc = bass.Bass("TRN2", target_bir_lowering=False)

    x_h = nc.dram_tensor("x", [TOK, E], F32, kind="ExternalInput")
    wqkv_h = nc.dram_tensor("wqkv", [E, 3 * E], F32, kind="ExternalInput")
    bqkv_h = nc.dram_tensor("bqkv", [3 * E], F32, kind="ExternalInput")
    w1_h = nc.dram_tensor("w1", [E, FF], F32, kind="ExternalInput")
    b1_h = nc.dram_tensor("b1", [FF], F32, kind="ExternalInput")
    w2_h = nc.dram_tensor("w2", [FF, E], F32, kind="ExternalInput")
    b2_h = nc.dram_tensor("b2", [E], F32, kind="ExternalInput")
    out_h = nc.dram_tensor("out", [TOK, E], F32, kind="ExternalOutput")

    pqsp_h = nc.dram_tensor("pqspill", [NAG, P, AG * P * EC], BF16)
    bmask_h = nc.dram_tensor("bmask", [P, 2], F32, kind="ExternalInput")
    ccin_h = nc.dram_tensor("cc_in", [P, 1040], F32)
    ccout_h = nc.dram_tensor("cc_out", [P, 1040], F32)

    idn_h = nc.inline_tensor(np.eye(P, dtype=ml_dtypes.bfloat16), name="idnc")
    ones_h = nc.inline_tensor(np.ones((1, 512), dtype=ml_dtypes.bfloat16), name="onesc")
    ones16_h = nc.inline_tensor(np.ones((P, H), dtype=ml_dtypes.bfloat16), name="ones16c")

    from contextlib import ExitStack
    with ExitStack() as stack:
        en = stack.enter_context
        WREG = en(nc.sbuf_tensor("WREG", [P, WREG_COLS], BF16))
        pq = en(nc.sbuf_tensor("pq", [P, 4096], BF16))
        lnf = en(nc.sbuf_tensor("lnf", [P, 4096], BF16))
        lntok = en(nc.sbuf_tensor("lntok", [P, E], BF16))
        xa = en(nc.sbuf_tensor("xa", [P, 4 * E], F32))
        S23 = en(nc.sbuf_tensor("S23", [P, E], F32))
        pk = en(nc.sbuf_tensor("pk", [P, E], BF16))
        vbuf = en(nc.sbuf_tensor("vbuf", [P, E], BF16))
        hfm = en(nc.sbuf_tensor("hfm", [P, FC * BG * P], BF16))
        b1pp = en(nc.sbuf_tensor("b1pp", [P, FC], F32))
        b1st = en(nc.sbuf_tensor("b1st", [FC, P], BF16))
        brows = en(nc.sbuf_tensor("brows", [1, 4 * E + 512], BF16))
        ones16 = en(nc.sbuf_tensor("ones16", [P, H], BF16))
        ccsb = en(nc.sbuf_tensor("ccsb", [P, 520], F32))
        bmask = en(nc.sbuf_tensor("bmsk", [P, 2], F32))
        kvacc = en(nc.sbuf_tensor("kvacc", [P, 1032], F32))
        kvb = en(nc.sbuf_tensor("kvb", [P, E], BF16))
        ksb = en(nc.sbuf_tensor("ksb", [P, P], BF16))
        idn = en(nc.sbuf_tensor("idn", [P, P], BF16))
        stats = en(nc.sbuf_tensor("stats", [P, 16], F32))
        zbuf = en(nc.sbuf_tensor("zbuf", [P, 2 * H], F32))
        KV1 = en(nc.psum_tensor("KV1", [P, 512], F32))
        KV2 = en(nc.psum_tensor("KV2", [P, 512], F32))
        P3 = en(nc.psum_tensor("P3", [P, 512], F32))
        P4 = en(nc.psum_tensor("P4", [P, 512], F32))
        PT = en(nc.psum_tensor("PT", [P, E], BF16))
        PA = en(nc.psum_tensor("PA", [P, 512], F32))
        PB = en(nc.psum_tensor("PB", [P, 512], F32))
        PDEN = en(nc.psum_tensor("PDEN", [P, H], F32))
        sems = {}
        for q in ("LX", "LW", "SP", "LV"):
            for i in range(Sched.NROT):
                sems[f"{q}{i}"] = en(nc.semaphore(f"{q}{i}"))
        for nm in ("CC", "T", "V", "S", "G"):
            sems[nm] = en(nc.semaphore(nm))
        block = en(nc.Block())
        kvt = [KV1, KV2, P3, P4]
        s = Sched()

        def dma(engine, out, in_, waits=(), wp=None, **kw):
            s.op(engine, lambda e, o=out, i=in_, kw=kw: e.dma_start(out=o, in_=i, **kw),
                 waits=waits, wp=wp, dma=True)

        prev_ln = [None, None]

        def emit_ln_v(tag, slot, data_gates, s23_gates, par=0):
            so = 8 * par
            xs = xa[:, E * slot: E * (slot + 1)]
            g0 = tuple(data_gates) + ((f"s_ln_{prev_ln[par]}",) if prev_ln[par] else ())
            prev_ln[par] = tag
            s.op("vector", lambda v: v.reduce_sum(stats[:, so + 0:so + 1], xs, axis=AX.X),
                 waits=g0, wp=f"v_sum_{tag}")
            s.op("vector", lambda v: v.tensor_scalar(stats[:, so + 2:so + 3], stats[:, so + 0:so + 1], 1.0 / E, None, ALU.mult))
            s.op("vector", lambda v: v.tensor_scalar(stats[:, so + 3:so + 4], stats[:, so + 0:so + 1], -1.0 / E, None, ALU.mult),
                 wp=f"v_means_{tag}")
            sqdst = hfm[:, 0:E] if tag.startswith("a") else S23[:, :]
            s.op("scalar", lambda sc, sqdst=sqdst: sc.activation(sqdst, xs, AF.Square, accum_out=stats[:, so + 1:so + 2]),
                 waits=tuple(data_gates) + tuple(s23_gates), wp=f"s_sq_{tag}")
            s.op("vector", lambda v: v.tensor_scalar(stats[:, so + 4:so + 5], stats[:, so + 1:so + 2], 1.0 / E, None, ALU.mult),
                 waits=(f"s_sq_{tag}",))
            s.op("vector", lambda v: v.tensor_tensor(stats[:, so + 5:so + 6], stats[:, so + 2:so + 3], stats[:, so + 2:so + 3], ALU.mult))
            s.op("vector", lambda v: v.tensor_tensor(stats[:, so + 4:so + 5], stats[:, so + 4:so + 5], stats[:, so + 5:so + 6], ALU.subtract))
            s.op("vector", lambda v: v.tensor_scalar(stats[:, so + 4:so + 5], stats[:, so + 4:so + 5], 1e-5, None, ALU.add),
                 wp=f"v_rcp_{tag}")

        def emit_ln_s(tag, slot, lntok_gate, buf=None, par=0):
            buf = lntok if buf is None else buf
            so = 8 * par
            xs = xa[:, E * slot: E * (slot + 1)]
            s.op("scalar", lambda sc: sc.activation(stats[:, so + 5:so + 6], stats[:, so + 4:so + 5], AF.Ln),
                 waits=(f"v_rcp_{tag}",))
            s.op("scalar", lambda sc: sc.activation(stats[:, so + 6:so + 7], stats[:, so + 5:so + 6], AF.Exp, scale=-0.5),
                 wp=f"s_istd_{tag}")
            s.op("scalar", lambda sc: sc.activation(stats[:, so + 7:so + 8], stats[:, so + 3:so + 4], AF.Identity,
                                                    scale=stats[:, so + 6:so + 7]),
                 waits=(f"s_istd_{tag}", f"v_means_{tag}"), wp=f"s_nbm_{tag}")
            s.op("scalar", lambda sc, buf=buf: sc.activation(buf[:, :], xs, AF.Identity,
                                                    scale=stats[:, so + 6:so + 7], bias=stats[:, so + 7:so + 8]),
                 waits=(f"s_nbm_{tag}",) + ((lntok_gate,) if lntok_gate else ()),
                 wp=f"s_ln_{tag}")

        def emit_ln(tag, slot, data_gates, lntok_gate, s23_gates, buf=None, par=0):
            emit_ln_v(tag, slot, data_gates, s23_gates, par=par)
            emit_ln_s(tag, slot, lntok_gate, buf=buf, par=par)

        prev_trans = [None]

        def emit_trans(tag, dst_off, dst_step, lnf_gate=None, buf=None, dst=None):
            buf = lntok if buf is None else buf
            dh, db = (lnf, 0) if dst is None else dst
            gates0 = [f"s_ln_{tag}", "ld_idn"]
            if prev_trans[0]:
                gates0.append(f"v_lnf_{prev_trans[0]}")
            else:
                gates0.append("v_b1pp")
            for c in range(EC):
                s.op("tensor", lambda t, c=c, buf=buf: t.transpose(PT[:, P * c: P * (c + 1)],
                                                          buf[:, P * c: P * (c + 1)], idn[:, :]),
                     waits=tuple(gates0) if c == 0 else (),
                     wp=(f"t_tr_{tag}" if c == EC - 1 else None))

            def _evac(v, dst_off=dst_off, dst_step=dst_step, dh=dh, db=db):
                dstap = bass.AP(dh, db + dst_off, [[dh.shape[1], P], [dst_step, EC], [1, P]])
                src = PT.ap().rearrange("p (c t) -> p c t", t=P)
                return v.tensor_copy(dstap, src)
            s.op("vector", _evac, waits=(f"t_tr_{tag}",) + ((lnf_gate,) if lnf_gate else ()),
                 wp=f"v_lnf_{tag}")
            prev_trans[0] = tag

        prev_phi = [None]

        def emit_phi(tag, pt, n, dst_ap, waits, scr=None):
            sb = scr if scr is not None else S23
            w0 = tuple(waits) + ((prev_phi[0],) if prev_phi[0] else ())
            prev_phi[0] = f"v_phi_{tag}"
            s.op("scalar", lambda sc: sc.activation(sb[:, 0:n], pt[:, 0:n], AF.Relu, scale=-1.0),
                 waits=w0, wp=f"s_mn_{tag}")
            s.op("scalar", lambda sc: sc.activation(sb[:, 512:512 + n], sb[:, 0:n], AF.Exp, scale=-1.0),
                 wp=f"s_ex_{tag}")
            s.op("vector", lambda v: v.scalar_tensor_tensor(dst_ap, pt[:, 0:n], 0.0,
                                                            sb[:, 512:512 + n], ALU.max, ALU.add),
                 waits=(f"s_ex_{tag}",), wp=f"v_phi_{tag}")

        def emit_kvmm(T):
            pkS = pk if T % 2 == 0 else hfm[:, E:2 * E]
            vbS = vbuf if T % 2 == 0 else hfm[:, 2 * E:3 * E]
            w0 = [f"v_phi_a{T}k0", f"v_phi_a{T}k1", f"v_vex_a{T}0", f"v_vex_a{T}1", "ld_ones16"]
            if T >= 1:
                w0.append(f"v_kvadd{T - 1}")
            for c in range(8):
                tns = KV1 if c < 4 else KV2
                col = P * (c % 4)
                s.op("tensor", lambda t, tns=tns, col=col, c=c:
                     t.matmul(tns[:, col: col + D],
                              pkS[:, P * c: P * (c + 1)],
                              vbS[:, D * 2 * c: D * 2 * c + D],
                              start=True, stop=True),
                     waits=tuple(w0) if c == 0 else ())
                s.op("tensor", lambda t, tns=tns, col=col, c=c:
                     t.matmul(tns[:, col + D: col + 2 * D],
                              pkS[:, P * c: P * (c + 1)],
                              vbS[:, D * (2 * c + 1): D * (2 * c + 1) + D],
                              start=True, stop=True))
                s.op("tensor", lambda t, c=c:
                     t.matmul(PDEN[:, c: c + 1],
                              pkS[:, P * c: P * (c + 1)],
                              ones16[:, 0:1],
                              start=True, stop=True),
                     wp=f"t_kvmm{T}" if c == 7 else None)
            for i, (srcp, acol, n) in enumerate([(KV1, 0, 512), (KV2, 512, 512), (PDEN, 1024, 8)]):
                s.op("vector", lambda v, srcp=srcp, acol=acol, n=n: v.tensor_tensor(
                    kvacc[:, acol: acol + n], kvacc[:, acol: acol + n],
                    srcp[:, 0:n], ALU.add),
                    waits=(f"t_kvmm{T}", "v_kvacc0") if i == 0 else (),
                    wp=f"v_kvadd{T}" if i == 2 else None,
                    nodrain=(i >= 1))

        dma("sync", idn[:, :], idn_h[:, :], wp="ld_idn")
        dma("sync", brows[0:1, 4 * E: 4 * E + 512], ones_h[:, :], wp="ld_ones")
        dma("sync", ones16[:, :], ones16_h[:, :], wp="ld_ones16")

        dma("sync", bmask[:, :], bmask_h[:, :], wp="ld_bmask")
        dma("gpsimd", brows[0:1, 0:3 * E], bqkv_h.ap().rearrange("(a n) -> a n", a=1), wp="ld_bq")
        dma("gpsimd", brows[0:1, 3 * E:4 * E], b2_h.ap().rearrange("(a n) -> a n", a=1), wp="ld_b2")
        dma("gpsimd", b1st[:, :], b1_h.ap().rearrange("(c p) -> c p", p=P), wp="ld_b1st")
        for e in range(EC):
            dma("gpsimd", WREG[:, WQ_OFF + 3072 * e + 1024: WQ_OFF + 3072 * (e + 1)],
                wqkv_h[P * e: P * (e + 1), 1024:3072], wp=f"ld_wkv{e}")
        for e in range(EC):
            dma("gpsimd", WREG[:, WQ_OFF + 3072 * e: WQ_OFF + 3072 * e + 1024],
                wqkv_h[P * e: P * (e + 1), 0:1024], wp=f"ld_wq{e}")

        s.op("tensor", lambda t: t.transpose(PT[:, 0:FC], b1st[:, :], idn[0:FC, 0:FC]),
             waits=("ld_b1st", "ld_idn"), wp="t_b1T")
        s.op("vector", lambda v: v.tensor_copy(b1pp[:, :], PT[:, 0:FC]),
             waits=("t_b1T",), wp="v_b1pp")

        s.op("vector", lambda v: v.memset(kvb[:, :], 0.0), wp="v_kvb0")
        s.op("vector", lambda v: v.memset(kvacc[:, :], 0.0), wp="v_kvacc0")
        s.op("vector", lambda v: v.memset(ksb[:, :], 0.0), wp="v_ksb0")

        def emit_ar_start():
            for h in range(H):
                cp, half = h // 2, h % 2
                kv_src = kvacc[64 * half: 64 * half + 64, P * cp + D * half: P * cp + D * half + D]
                kv_dst = ccsb[64 * half: 64 * half + 64, 65 * cp: 65 * cp + D]
                s.op("vector", lambda v, kv_src=kv_src, kv_dst=kv_dst: v.tensor_copy(kv_dst, kv_src),
                     waits=(f"v_kvadd{NT - 1}",) if h == 0 else (),
                     wp=None, nodrain=(h >= 1))
                ks_src = kvacc[64 * half: 64 * half + 64, 1024 + cp: 1024 + cp + 1]
                ks_dst = ccsb[64 * half: 64 * half + 64, 65 * cp + D: 65 * cp + D + 1]
                s.op("vector", lambda v, ks_src=ks_src, ks_dst=ks_dst: v.tensor_copy(ks_dst, ks_src),
                     wp=f"v_cc{h}", nodrain=True)
            ccfull = xa[:, 2 * E: 2 * E + 1040]
            s.op("vector", lambda v, ccfull=ccfull: v.tensor_scalar(
                ccfull[:, 0:520], ccsb[:, :], bmask[:, 0:1], None, ALU.mult),
                waits=(f"v_cc{H - 1}", "ld_bmask", f"s_ln_a{NT - 1}"))
            s.op("vector", lambda v, ccfull=ccfull: v.tensor_scalar(
                ccfull[:, 520:1040], ccsb[:, :], bmask[:, 1:2], None, ALU.mult),
                wp="v_ccfull", nodrain=True)
            dma("scalar", ccin_h[:, :], ccfull[:, :], waits=("v_ccfull",), wp="g_ccin")
            s.op("gpsimd", lambda g: g.collective_compute(
                "AllReduce", ALU.add, replica_groups=[[0, 1, 2, 3, 4, 5, 6, 7]],
                ins=[ccin_h.ap().opt()], outs=[ccout_h.ap().opt()]),
                waits=("g_ccin",), cc=True, wp="cc_done")


        def abuf(T):
            return lntok if T % 2 == 0 else hfm[:, 3 * E:4 * E]

        def albuf(g):
            return lnf if g % 2 == 0 else hfm[:, 4 * E:8 * E]

        dma("sync", xa[:, 0:E], x_h[0:P, :], wp="ld_x_a0")
        emit_ln_v("a0", 0, ("ld_x_a0",), (), par=0)
        emit_ln_s("a0", 0, None, buf=abuf(0), par=0)

        def emit_q(g, after_q=False):
            LH, LB = (lnf, 0) if g % 2 == 0 else (hfm, 4 * E)
            lastT = min(AG * (g + 1) + AG - 1, NT - 1)
            for fc in range(EC):
                pt = [PA, PB, P3, P4][fc % 4]
                wts = [f"v_lnf_a{AG * g + AG - 1}"] + [f"ld_wq{e}" for e in range(EC)]
                if fc >= 4:
                    wts += [f"v_phi_q{g}_{fc - 4}", f"s_ex_q{g}_{fc - 4}"]
                elif after_q:
                    wts += [f"v_phi_q{g - 1}_{fc + 4}", f"s_ex_q{g - 1}_{fc + 4}"]
                elif fc < 2:
                    wts += [f"v_phi_a{lastT}k{fc}", f"s_ex_a{lastT}k{fc}"]
                else:
                    wts.append(f"v_vex_a{lastT}{fc - 2}")
                for e in range(EC):
                    s.op("tensor", lambda t, e=e, fc=fc, pt=pt, LH=LH, LB=LB:
                         t.matmul(pt[:, :],
                                  WREG[:, WQ_OFF + 3072 * e + P * fc: WQ_OFF + 3072 * e + P * (fc + 1)],
                                  LH[:, LB + AG * P * e: LB + AG * P * (e + 1)],
                                  start=(e == 0), stop=False),
                         waits=tuple(wts) if e == 0 else ())
                s.op("tensor", lambda t, fc=fc, pt=pt:
                     t.matmul(pt[:, :], brows[0:1, P * fc: P * (fc + 1)],
                              brows[0:1, 4 * E: 4 * E + 512],
                              start=False, stop=True),
                     wp=f"t_q{g}_{fc}")
                phw = [f"t_q{g}_{fc}"]
                if g >= 1 and fc == 0:
                    phw.append(f"sp_spill{g - 1}")
                emit_phi(f"q{g}_{fc}", pt, 512, pq[:, 512 * fc: 512 * (fc + 1)], phw)
            dma("scalar", pqsp_h[g, :, :], pq[:, :],
                waits=(f"v_phi_q{g}_{EC - 1}",), wp=f"sp_spill{g}")

        for g in range(NAG):
            LH, LB = (lnf, 0) if g % 2 == 0 else (hfm, 4 * E)
            for j in range(AG):
                T = AG * g + j
                slot = T % 4
                tag = f"a{T}"
                if T + 1 < NT:
                    Tn = T + 1
                    dma("sync", xa[:, E * (Tn % 4): E * (Tn % 4 + 1)], x_h[P * Tn: P * (Tn + 1), :],
                        waits=((f"s_ln_a{Tn - 4}",) if Tn >= 4 else ()), wp=f"ld_x_a{Tn}")
                    emit_ln_v(f"a{Tn}", Tn % 4, (f"ld_x_a{Tn}",), (), par=Tn % 2)

                emit_trans(tag, dst_off=P * j, dst_step=AG * P,
                           lnf_gate=(f"t_q{g - 2}_{EC - 1}" if (j == 0 and g >= 2) else None),
                           buf=abuf(T), dst=(LH, LB))

                banks = [PA, PB, P3, P4]
                pkT = pk if T % 2 == 0 else hfm[:, E:2 * E]
                vbT = vbuf if T % 2 == 0 else hfm[:, 2 * E:3 * E]
                kw = [f"v_lnf_{tag}", "ld_bq", "ld_ones"]
                vw0 = []
                if T >= 1:
                    vw0 += [f"v_vex_a{T - 1}0", f"v_vex_a{T - 1}1"]
                if j >= 1 or g == 0:
                    if T >= 1:
                        kw += [f"v_phi_a{T - 1}k0", f"v_phi_a{T - 1}k1", f"s_ex_a{T - 1}k0", f"s_ex_a{T - 1}k1"]
                else:
                    kw += [f"v_phi_q{g - 2}_4", f"v_phi_q{g - 2}_5"] if g >= 2 else                           [f"v_phi_a{T - 1}k0", f"v_phi_a{T - 1}k1"]
                    if g >= 2:
                        vw0 += [f"v_phi_q{g - 2}_6", f"v_phi_q{g - 2}_7", f"s_ex_q{g - 2}_6", f"s_ex_q{g - 2}_7"]
                for half in range(2):
                    ci0 = 2 * half
                    for e in range(EC):
                        for dci in range(2):
                            ci = ci0 + dci
                            col = 1024 + 512 * ci
                            w = []
                            if ci == ci0 and e == 0:
                                w = kw if half == 0 else vw0
                            if dci == 0:
                                w = list(w) + [f"ld_wkv{e}"]
                            s.op("tensor", lambda t, e=e, ci=ci, col=col, j=j, LH=LH, LB=LB:
                                 t.matmul(banks[ci][:, :],
                                          LH[:, LB + AG * P * e + P * j: LB + AG * P * e + P * (j + 1)],
                                          WREG[:, WQ_OFF + 3072 * e + col: WQ_OFF + 3072 * e + col + 512],
                                          start=(e == 0), stop=False),
                                 waits=tuple(w))
                    for dci in range(2):
                        ci = ci0 + dci
                        cn = ["k0", "k1", "v0", "v1"][ci]
                        col = 1024 + 512 * ci
                        s.op("tensor", lambda t, ci=ci, col=col:
                             t.matmul(banks[ci][:, :], brows[0:1, 4 * E: 4 * E + P],
                                      brows[0:1, col: col + 512], start=False, stop=True),
                             wp=f"t_{cn}_{tag}")
                    if half == 0:
                        for ci in (0, 1):
                            phw2 = [f"t_k{ci}_{tag}"]
                            if T >= 2:
                                phw2.append(f"t_kvmm{T - 2}")
                            emit_phi(f"{tag}k{ci}", banks[ci], 512,
                                     pkT[:, 512 * ci: 512 * (ci + 1)],
                                     tuple(phw2))
                        if T + 1 < NT:
                            emit_ln_s(f"a{T + 1}", (T + 1) % 4, f"t_tr_a{T - 1}" if T >= 1 else None,
                                      buf=abuf(T + 1), par=(T + 1) % 2)
                    else:
                        for ci in (2, 3):
                            vw = [f"t_v{ci - 2}_{tag}"]
                            if T >= 2:
                                vw.append(f"t_kvmm{T - 2}")
                            s.op("vector", lambda v, ci=ci, vbT=vbT: v.tensor_copy(
                                vbT[:, 512 * (ci - 2): 512 * (ci - 1)], banks[ci][:, :]),
                                waits=tuple(vw), wp=f"v_vex_{tag}{ci - 2}")
                if T >= 1:
                    emit_kvmm(T - 1)

            if g >= 1:
                emit_q(g - 1)
            if g == 0:
                for e in range(EC):
                    dma("gpsimd", WREG[:, W1_OFF + FF * e: W1_OFF + FF * (e + 1)],
                        w1_h[P * e: P * (e + 1), :], wp=f"ld_w1{e}")
            if g == 1:
                for q4 in range(2):
                    dst = WREG.ap().rearrange("p (c e) -> p c e", e=E)[
                        :, (W2_OFF // E) + 4 * q4: (W2_OFF // E) + 4 * (q4 + 1), :]
                    src2 = w2_h.ap().rearrange("(c p) e -> p c e", p=P)[:, 4 * q4: 4 * (q4 + 1), :]
                    dma("gpsimd", dst, src2, wp=f"ld_w2h{q4}")

        emit_kvmm(NT - 1)
        emit_ar_start()
        emit_q(NAG - 1, after_q=True)

        ccfull2 = xa[:, 2 * E: 2 * E + 1040]
        dma("scalar", ccfull2, ccout_h[:, :], waits=("cc_done",), wp="g_ccout_pre")
        s.op("vector", lambda v: v.tensor_scalar(
            ccsb[:, :], ccfull2[:, 0:520], bmask[:, 0:1], None, ALU.mult),
            waits=("g_ccout_pre",))
        s.op("vector", lambda v: v.scalar_tensor_tensor(
            ccsb[:, :], ccfull2[:, 520:1040], bmask[:, 1:2], ccsb[:, :], ALU.mult, ALU.add),
            wp="g_ccout")
        for q4 in range(6):
            dst = WREG.ap().rearrange("p (c e) -> p c e", e=E)[
                :, (W2_OFF // E) + 8 + 4 * q4: (W2_OFF // E) + 8 + 4 * (q4 + 1), :]
            src = w2_h.ap().rearrange("(c p) e -> p c e", p=P)[:, 8 + 4 * q4: 8 + 4 * (q4 + 1), :]
            dma("gpsimd", dst, src,
                waits=(f"t_q{NAG - 1}_{EC - 1}",) if q4 == 0 else (), wp=f"ld_w2t{q4}")

        for h in range(H):
            half, c2 = h % 2, h // 2
            s.op("vector", lambda v, half=half, c2=c2: v.tensor_copy(
                kvb[64 * half: 64 * half + 64, P * c2 + D * half: P * c2 + D * half + D],
                ccsb[64 * half: 64 * half + 64, 65 * c2: 65 * c2 + D]),
                waits=("g_ccout", "v_kvb0", "v_ksb0") if h == 0 else (), nodrain=(h >= 1))
            s.op("vector", lambda v, h=h, half=half, c2=c2: v.tensor_copy(
                ksb[64 * half: 64 * half + 64, H * c2 + h: H * c2 + h + 1],
                ccsb[64 * half: 64 * half + 64, 65 * c2 + D: 65 * c2 + D + 1]),
                wp="v_kvbdone" if h == H - 1 else None, nodrain=True)

        def emit_attn(bg):
            hb = bg % 2
            gsrc = bg // 2
            rw = [f"sp_spill{NAG - 1}"]
            if bg >= 2:
                rw.append(f"t_attn{2 * (bg - 2) + 1}")
            pq_dst = pq.ap()[:, 2048 * hb: 2048 * (hb + 1)].rearrange("p (c t) -> p c t", t=256)
            pq_src = pqsp_h.ap()[gsrc, :, :].rearrange("p (c t) -> p c t", t=AG * P)[
                :, :, 256 * (bg % 2): 256 * (bg % 2) + 256]
            dma("sync", pq_dst, pq_src, waits=tuple(rw), wp=f"ld_pq{bg}")

            for jj in range(BG):
                j = BG * bg + jj
                slot = j % 4
                tag = f"b{j}"
                gate = (f"sp_out{j - 4}",) if j >= 4 else (f"s_ln_a{12 + slot}",)
                if j < 4 and slot >= 2:
                    gate = gate + ("g_ccout",)
                dma("sync", xa[:, E * slot: E * (slot + 1)], x_h[P * j: P * (j + 1), :],
                    waits=gate, wp=f"ld_x_{tag}")

                aw = [f"ld_pq{bg}", "v_kvbdone"]
                if j >= 1:
                    aw.append(f"v_az_{j - 1}")
                else:
                    aw += [f"v_phi_q{NAG - 1}_{EC - 2}", f"v_phi_q{NAG - 1}_{EC - 1}",
                           f"s_ex_q{NAG - 1}_{EC - 2}", f"s_ex_q{NAG - 1}_{EC - 1}"]
                for c in range(EC):
                    pt = P3 if c < 4 else P4
                    s.op("tensor", lambda t, c=c, pt=pt, hb=hb, jj=jj:
                         t.matmul(pt[:, P * (c % 4): P * (c % 4 + 1)],
                                  pq[:, 2048 * hb + 256 * c + P * jj: 2048 * hb + 256 * c + P * (jj + 1)],
                                  kvb[:, P * c: P * (c + 1)], start=True, stop=True),
                         waits=tuple(aw) if c == 0 else ())
                    s.op("tensor", lambda t, c=c, hb=hb, jj=jj:
                         t.matmul(PDEN[:, :],
                                  pq[:, 2048 * hb + 256 * c + P * jj: 2048 * hb + 256 * c + P * (jj + 1)],
                                  ksb[:, H * c: H * (c + 1)],
                                  start=(c == 0), stop=(c == EC - 1)),
                         wp=(f"t_attn{j}" if c == EC - 1 else None))

                s.op("vector", lambda v: v.tensor_scalar(zbuf[:, 0:H], PDEN[:, :], 1e-6, None, ALU.add),
                     waits=(f"t_attn{j}", f"ld_x_{tag}"))
                s.op("vector", lambda v: v.reciprocal(zbuf[:, H:2 * H], zbuf[:, 0:H]), wp=f"v_z_{j}")
                for h in range(H):
                    pt = P3 if h < 8 else P4
                    s.op("vector", lambda v, h=h, pt=pt, slot=slot: v.scalar_tensor_tensor(
                        xa[:, E * slot + D * h: E * slot + D * (h + 1)],
                        pt[:, D * (h % 8): D * (h % 8 + 1)],
                        zbuf[:, H + h: H + h + 1],
                        xa[:, E * slot + D * h: E * slot + D * (h + 1)],
                        ALU.mult, ALU.add),
                        waits=(f"v_z_{j}",) if h == 0 else (),
                        wp=f"v_az_{j}" if h == H - 1 else None,
                        nodrain=(h >= 1))

        def emit_ln2(bg):
            for jj in range(BG):
                j = BG * bg + jj
                tag = f"b{j}"
                if j >= 2:
                    lg = f"t_tr_b{j - 2}"
                elif jj == 0:
                    lg = f"t_tr_a{NT - 1}"
                else:
                    lg = f"t_kvmm{NT - 1}"
                emit_ln(tag, j % 4, (f"v_az_{j}",), lg, (),
                        buf=(lntok if jj == 0 else pk), par=j % 2)

        def emit_trans2(bg):
            hb = bg % 2
            for jj in range(BG):
                emit_trans(f"b{BG * bg + jj}", dst_off=2048 * hb + P * jj, dst_step=BG * P,
                           lnf_gate=(f"t_f1{bg - 2}_{FC - 1}" if bg >= 2 else None),
                           buf=(lntok if jj == 0 else pk))

        def emit_ffn1(bg):
            hb = bg % 2
            for f in range(FC):
                pt = KV1 if f % 2 == 0 else KV2
                wts = [f"v_lnf_b{BG * bg + BG - 1}"] + [f"ld_w1{e}" for e in range(EC)]
                if bg == 0 and f < 2:
                    wts.append(f"v_kvadd{NT - 1}")
                if f >= 2:
                    wts.append(f"s_gel{bg}_{f - 2}")
                elif bg >= 1:
                    wts.append(f"s_gel{bg - 1}_{FC - 2 + f}")
                for e in range(EC):
                    s.op("tensor", lambda t, e=e, f=f, pt=pt, hb=hb:
                         t.matmul(pt[:, 0:256],
                                  WREG[:, W1_OFF + FF * e + P * f: W1_OFF + FF * e + P * (f + 1)],
                                  lnf[:, 2048 * hb + BG * P * e: 2048 * hb + BG * P * (e + 1)],
                                  start=(e == 0), stop=(e == EC - 1)),
                         waits=tuple(wts) if e == 0 else (),
                         wp=f"t_f1{bg}_{f}" if e == EC - 1 else None)
                gw = [f"t_f1{bg}_{f}", "v_b1pp"]
                if bg >= 1 and f == 0:
                    gw.append(f"t_f2{bg - 1}_{BG - 1}")
                s.op("scalar", lambda sc, f=f, pt=pt: sc.activation(
                    hfm[:, 256 * f: 256 * (f + 1)], pt[:, 0:256], AF.Gelu, bias=b1pp[:, f:f + 1]),
                    waits=tuple(gw), wp=f"s_gel{bg}_{f}")

        def emit_ffn2(bg):
            for jj in range(BG):
                j = BG * bg + jj
                wts = [f"s_gel{bg}_{FC - 1}", f"s_ln_b{j}"]
                if bg == 0:
                    wts += ["ld_w2h0", "ld_w2h1"]
                if jj == 0:
                    if bg >= 1:
                        wts += [f"f2evac{bg - 1}_1_0", f"f2evac{bg - 1}_1_1"]
                    else:
                        wts += [f"v_phi_q{NAG - 1}_{EC - 2}", f"v_phi_q{NAG - 1}_{EC - 1}",
                                f"s_ex_q{NAG - 1}_{EC - 2}", f"s_ex_q{NAG - 1}_{EC - 1}"]
                else:
                    wts += [f"f2evac{bg}_0_0", f"f2evac{bg}_0_1"]
                for f in range(FC):
                    fw = list(wts) if f == 0 else []
                    if bg == 0 and f >= 8 and (f - 8) % 4 == 0:
                        fw.append(f"ld_w2t{(f - 8) // 4}")
                    s.op("tensor", lambda t, f=f, jj=jj:
                         t.matmul(PA[:, :], hfm[:, 256 * f + P * jj: 256 * f + P * (jj + 1)],
                                  WREG[:, W2_OFF + E * f: W2_OFF + E * f + 512],
                                  start=(f == 0), stop=False),
                         waits=tuple(fw))
                    s.op("tensor", lambda t, f=f, jj=jj:
                         t.matmul(PB[:, :], hfm[:, 256 * f + P * jj: 256 * f + P * (jj + 1)],
                                  WREG[:, W2_OFF + E * f + 512: W2_OFF + E * f + 1024],
                                  start=(f == 0), stop=False))
                s.op("tensor", lambda t:
                     t.matmul(PA[:, :], brows[0:1, 4 * E: 4 * E + P],
                              brows[0:1, 3 * E: 3 * E + 512], start=False, stop=True))
                s.op("tensor", lambda t:
                     t.matmul(PB[:, :], brows[0:1, 4 * E: 4 * E + P],
                              brows[0:1, 3 * E + 512: 4 * E], start=False, stop=True),
                     wp=f"t_f2{bg}_{jj}")
                for ec in range(2):
                    def _f2ev(v, slot=j % 4, ec=ec):
                        xsl = xa[:, E * slot + 512 * ec: E * slot + 512 * (ec + 1)]
                        return v.tensor_tensor(xsl, xsl, (PA if ec == 0 else PB)[:, :], ALU.add)
                    s.op("vector", _f2ev, waits=(f"t_f2{bg}_{jj}",) if ec == 0 else (),
                         wp=f"f2evac{bg}_{jj}_{ec}", nodrain=(ec == 1))
                slot = j % 4
                dma("scalar", out_h[P * j: P * (j + 1), :], xa[:, E * slot: E * (slot + 1)],
                    waits=(f"f2evac{bg}_{jj}_0", f"f2evac{bg}_{jj}_1"), wp=f"sp_out{j}")

        for bg in range(NBG):
            emit_attn(bg)
            if bg >= 1:
                emit_ffn1(bg - 1)
            emit_ln2(bg)
            if bg >= 1:
                emit_ffn2(bg - 1)
            emit_trans2(bg)
        emit_ffn1(NBG - 1)
        emit_ffn2(NBG - 1)

        s.op("scalar", lambda sc: sc.nop(), waits=(f"sp_out{NT - 1}",))

        @block.sync
        def _(e):
            s.emit("sync", e, sems)

        @block.gpsimd
        def _(e):
            s.emit("gpsimd", e, sems)

        @block.tensor
        def _(e):
            s.emit("tensor", e, sems)

        @block.vector
        def _(e):
            s.emit("vector", e, sems)

        @block.scalar
        def _(e):
            s.emit("scalar", e, sems)

    return nc


def kernel(L0_attn_x, L0_attn_W_qkv, L0_attn_b_qkv, L0_attn_W_proj, L0_attn_b_proj,
           L0_ffn_w1, L0_ffn_b1, L0_ffn_w2, L0_ffn_b2):
    x = np.ascontiguousarray(np.asarray(L0_attn_x, dtype=np.float32))
    wqkv = np.ascontiguousarray(np.asarray(L0_attn_W_qkv, dtype=np.float32))
    bqkv = np.ascontiguousarray(np.asarray(L0_attn_b_qkv, dtype=np.float32))
    w1 = np.ascontiguousarray(np.asarray(L0_ffn_w1, dtype=np.float32))
    b1 = np.ascontiguousarray(np.asarray(L0_ffn_b1, dtype=np.float32))
    w2 = np.ascontiguousarray(np.asarray(L0_ffn_w2, dtype=np.float32))
    b2 = np.ascontiguousarray(np.asarray(L0_ffn_b2, dtype=np.float32))

    nc = build_nc()
    in_maps = []
    for core in range(8):
        b = core // 4
        t0 = TOK * (core % 4)
        bm = np.zeros((128, 2), dtype=np.float32)
        bm[:, b] = 1.0
        in_maps.append({
            "x": np.ascontiguousarray(x[b, t0:t0 + TOK, :]),
            "wqkv": wqkv, "bqkv": bqkv, "w1": w1, "b1": b1, "w2": w2, "b2": b2,
            "bmask": bm,
        })
    res = run_bass_kernel_spmd(nc, in_maps, core_ids=list(range(8)))
    out = np.empty((2, 8192, 1024), dtype=np.float32)
    for core in range(8):
        b = core // 4
        t0 = TOK * (core % 4)
        out[b, t0:t0 + TOK, :] = res.results[core]["out"]
    return out


# version 14
# speedup vs baseline: 1.0101x; 1.0101x over previous
import numpy as np
import ml_dtypes
import concourse.bass as bass
import concourse.mybir as mybir
from concourse.bass_utils import run_bass_kernel_spmd

F32 = mybir.dt.float32
BF16 = mybir.dt.bfloat16
AF = mybir.ActivationFunctionType
ALU = mybir.AluOpType
AX = mybir.AxisListType

P = 128
TOK = 2048
NT = TOK // P
E = 1024
EC = E // P
H = 16
D = 64
FF = 4096
FC = FF // P
AG = 4
BG = 2
NAG = NT // AG
NBG = NT // BG

W1_OFF = 0
W2_OFF = 32768
WQ_OFF = 40960
WREG_COLS = 65536
VEXW = H * (D + 1)


class Sched:

    NROT = 6
    ENGQ = {"sync": "LX", "gpsimd": "LW", "scalar": "SP", "vector": "LV"}
    ESEM = {"tensor": "T", "vector": "V", "scalar": "S", "gpsimd": "G"}

    def __init__(self):
        self.ops = []
        self.cnt = {}
        self.wp = {}
        self.qidx = {}

    def _bump(self, sem, amt):
        self.cnt[sem] = self.cnt.get(sem, 0) + amt
        return self.cnt[sem]

    def op(self, engine, fn, waits=(), wp=None, dma=False, cc=False, nodrain=False):
        wl = []
        for w in waits:
            if w is None:
                continue
            assert w in self.wp, f"waypoint {w!r} not defined yet"
            wl.append(self.wp[w])
        inc = None
        if dma:
            q = self.ENGQ[engine]
            i = self.qidx.get(q, 0)
            self.qidx[q] = i + 1
            sem = f"{q}{i % self.NROT}"
            prev = self.cnt.get(sem, 0)
            if prev:
                wl.append((sem, prev))
            v = self._bump(sem, 16)
            inc = (sem, 16)
            if wp:
                self.wp[wp] = (sem, v)
        elif cc:
            v = self._bump("CC", 1)
            inc = ("CC", 1)
            if wp:
                self.wp[wp] = ("CC", v)
        elif wp:
            sem = self.ESEM[engine]
            v = self._bump(sem, 1)
            inc = (sem, 1)
            self.wp[wp] = (sem, v)
        drain = (not dma) and (not cc) and (not nodrain) and engine in ("vector", "scalar")
        self.ops.append((engine, fn, wl, inc, drain))

    def emit(self, eng_name, eng, sems):
        last = {}
        for engine, fn, waits, inc, drain in self.ops:
            if engine != eng_name:
                continue
            for sem, val in waits:
                if last.get(sem, 0) >= val:
                    continue
                eng.wait_ge(sems[sem], val)
                last[sem] = val
            if drain:
                eng.drain()
            instr = fn(eng)
            if inc is not None:
                assert instr is not None
                instr.then_inc(sems[inc[0]], inc[1])


def build_nc():
    nc = bass.Bass("TRN2", target_bir_lowering=False)

    x_h = nc.dram_tensor("x", [TOK, E], F32, kind="ExternalInput")
    wqkv_h = nc.dram_tensor("wqkv", [E, 3 * E], F32, kind="ExternalInput")
    bqkv_h = nc.dram_tensor("bqkv", [3 * E], F32, kind="ExternalInput")
    w1_h = nc.dram_tensor("w1", [E, FF], F32, kind="ExternalInput")
    b1_h = nc.dram_tensor("b1", [FF], F32, kind="ExternalInput")
    w2_h = nc.dram_tensor("w2", [FF, E], F32, kind="ExternalInput")
    b2_h = nc.dram_tensor("b2", [E], F32, kind="ExternalInput")
    out_h = nc.dram_tensor("out", [TOK, E], F32, kind="ExternalOutput")

    pqsp_h = nc.dram_tensor("pqspill", [NAG, P, AG * P * EC], BF16)
    bmask_h = nc.dram_tensor("bmask", [P, 2], F32, kind="ExternalInput")
    ccin_h = nc.dram_tensor("cc_in", [P, 1040], F32)
    ccout_h = nc.dram_tensor("cc_out", [P, 1040], F32)
    ccin2_h = nc.dram_tensor("cc_in2", [P, 520], F32)
    ccout2_h = nc.dram_tensor("cc_out2", [P, 520], F32)

    idn_h = nc.inline_tensor(np.eye(P, dtype=ml_dtypes.bfloat16), name="idnc")
    ones_h = nc.inline_tensor(np.ones((1, 512), dtype=ml_dtypes.bfloat16), name="onesc")
    ones16_h = nc.inline_tensor(np.ones((P, H), dtype=ml_dtypes.bfloat16), name="ones16c")

    from contextlib import ExitStack
    with ExitStack() as stack:
        en = stack.enter_context
        WREG = en(nc.sbuf_tensor("WREG", [P, WREG_COLS], BF16))
        pq = en(nc.sbuf_tensor("pq", [P, 4096], BF16))
        lnf = en(nc.sbuf_tensor("lnf", [P, 4096], BF16))
        lntok = en(nc.sbuf_tensor("lntok", [P, E], BF16))
        xa = en(nc.sbuf_tensor("xa", [P, 4 * E], F32))
        S23 = en(nc.sbuf_tensor("S23", [P, E], F32))
        pk = en(nc.sbuf_tensor("pk", [P, E], BF16))
        vbuf = en(nc.sbuf_tensor("vbuf", [P, E], BF16))
        hfm = en(nc.sbuf_tensor("hfm", [P, FC * BG * P], BF16))
        b1pp = en(nc.sbuf_tensor("b1pp", [P, FC], F32))
        b1st = en(nc.sbuf_tensor("b1st", [FC, P], BF16))
        brows = en(nc.sbuf_tensor("brows", [1, 4 * E + 512], BF16))
        ones16 = en(nc.sbuf_tensor("ones16", [P, H], BF16))
        ccsb = en(nc.sbuf_tensor("ccsb", [P, 520], F32))
        bmask = en(nc.sbuf_tensor("bmsk", [P, 2], F32))
        kvacc = en(nc.sbuf_tensor("kvacc", [P, 1032], F32))
        kvb = en(nc.sbuf_tensor("kvb", [P, E], BF16))
        ksb = en(nc.sbuf_tensor("ksb", [P, P], BF16))
        idn = en(nc.sbuf_tensor("idn", [P, P], BF16))
        stats = en(nc.sbuf_tensor("stats", [P, 16], F32))
        zbuf = en(nc.sbuf_tensor("zbuf", [P, 2 * H], F32))
        KV1 = en(nc.psum_tensor("KV1", [P, 512], F32))
        KV2 = en(nc.psum_tensor("KV2", [P, 512], F32))
        P3 = en(nc.psum_tensor("P3", [P, 512], F32))
        P4 = en(nc.psum_tensor("P4", [P, 512], F32))
        PT = en(nc.psum_tensor("PT", [P, E], BF16))
        PA = en(nc.psum_tensor("PA", [P, 512], F32))
        PB = en(nc.psum_tensor("PB", [P, 512], F32))
        PDEN = en(nc.psum_tensor("PDEN", [P, H], F32))
        sems = {}
        for q in ("LX", "LW", "SP", "LV"):
            for i in range(Sched.NROT):
                sems[f"{q}{i}"] = en(nc.semaphore(f"{q}{i}"))
        for nm in ("CC", "T", "V", "S", "G"):
            sems[nm] = en(nc.semaphore(nm))
        block = en(nc.Block())
        kvt = [KV1, KV2, P3, P4]
        s = Sched()

        def dma(engine, out, in_, waits=(), wp=None, **kw):
            s.op(engine, lambda e, o=out, i=in_, kw=kw: e.dma_start(out=o, in_=i, **kw),
                 waits=waits, wp=wp, dma=True)

        prev_ln = [None, None]

        def emit_ln_v(tag, slot, data_gates, s23_gates, par=0):
            so = 8 * par
            xs = xa[:, E * slot: E * (slot + 1)]
            g0 = tuple(data_gates) + ((f"s_ln_{prev_ln[par]}",) if prev_ln[par] else ())
            prev_ln[par] = tag
            s.op("vector", lambda v: v.reduce_sum(stats[:, so + 0:so + 1], xs, axis=AX.X),
                 waits=g0, wp=f"v_sum_{tag}")
            s.op("vector", lambda v: v.tensor_scalar(stats[:, so + 2:so + 3], stats[:, so + 0:so + 1], 1.0 / E, None, ALU.mult))
            s.op("vector", lambda v: v.tensor_scalar(stats[:, so + 3:so + 4], stats[:, so + 0:so + 1], -1.0 / E, None, ALU.mult),
                 wp=f"v_means_{tag}")
            sqdst = hfm[:, 0:E] if tag.startswith("a") else S23[:, :]
            s.op("scalar", lambda sc, sqdst=sqdst: sc.activation(sqdst, xs, AF.Square, accum_out=stats[:, so + 1:so + 2]),
                 waits=tuple(data_gates) + tuple(s23_gates), wp=f"s_sq_{tag}")
            s.op("vector", lambda v: v.tensor_scalar(stats[:, so + 4:so + 5], stats[:, so + 1:so + 2], 1.0 / E, None, ALU.mult),
                 waits=(f"s_sq_{tag}",))
            s.op("vector", lambda v: v.tensor_tensor(stats[:, so + 5:so + 6], stats[:, so + 2:so + 3], stats[:, so + 2:so + 3], ALU.mult))
            s.op("vector", lambda v: v.tensor_tensor(stats[:, so + 4:so + 5], stats[:, so + 4:so + 5], stats[:, so + 5:so + 6], ALU.subtract))
            s.op("vector", lambda v: v.tensor_scalar(stats[:, so + 4:so + 5], stats[:, so + 4:so + 5], 1e-5, None, ALU.add),
                 wp=f"v_rcp_{tag}")

        def emit_ln_s(tag, slot, lntok_gate, buf=None, par=0):
            buf = lntok if buf is None else buf
            so = 8 * par
            xs = xa[:, E * slot: E * (slot + 1)]
            s.op("scalar", lambda sc: sc.activation(stats[:, so + 5:so + 6], stats[:, so + 4:so + 5], AF.Ln),
                 waits=(f"v_rcp_{tag}",))
            s.op("scalar", lambda sc: sc.activation(stats[:, so + 6:so + 7], stats[:, so + 5:so + 6], AF.Exp, scale=-0.5),
                 wp=f"s_istd_{tag}")
            s.op("scalar", lambda sc: sc.activation(stats[:, so + 7:so + 8], stats[:, so + 3:so + 4], AF.Identity,
                                                    scale=stats[:, so + 6:so + 7]),
                 waits=(f"s_istd_{tag}", f"v_means_{tag}"), wp=f"s_nbm_{tag}")
            s.op("scalar", lambda sc, buf=buf: sc.activation(buf[:, :], xs, AF.Identity,
                                                    scale=stats[:, so + 6:so + 7], bias=stats[:, so + 7:so + 8]),
                 waits=(f"s_nbm_{tag}",) + ((lntok_gate,) if lntok_gate else ()),
                 wp=f"s_ln_{tag}")

        def emit_ln(tag, slot, data_gates, lntok_gate, s23_gates, buf=None, par=0):
            emit_ln_v(tag, slot, data_gates, s23_gates, par=par)
            emit_ln_s(tag, slot, lntok_gate, buf=buf, par=par)

        prev_trans = [None]

        def emit_trans(tag, dst_off, dst_step, lnf_gate=None, buf=None, dst=None):
            buf = lntok if buf is None else buf
            dh, db = (lnf, 0) if dst is None else dst
            gates0 = [f"s_ln_{tag}", "ld_idn"]
            if prev_trans[0]:
                gates0.append(f"v_lnf_{prev_trans[0]}")
            else:
                gates0.append("v_b1pp")
            for c in range(EC):
                s.op("tensor", lambda t, c=c, buf=buf: t.transpose(PT[:, P * c: P * (c + 1)],
                                                          buf[:, P * c: P * (c + 1)], idn[:, :]),
                     waits=tuple(gates0) if c == 0 else (),
                     wp=(f"t_tr_{tag}" if c == EC - 1 else None))

            def _evac(v, dst_off=dst_off, dst_step=dst_step, dh=dh, db=db):
                dstap = bass.AP(dh, db + dst_off, [[dh.shape[1], P], [dst_step, EC], [1, P]])
                src = PT.ap().rearrange("p (c t) -> p c t", t=P)
                return v.tensor_copy(dstap, src)
            s.op("vector", _evac, waits=(f"t_tr_{tag}",) + ((lnf_gate,) if lnf_gate else ()),
                 wp=f"v_lnf_{tag}")
            prev_trans[0] = tag

        prev_phi = [None]

        def emit_phi(tag, pt, n, dst_ap, waits, scr=None):
            sb = scr if scr is not None else S23
            w0 = tuple(waits) + ((prev_phi[0],) if prev_phi[0] else ())
            prev_phi[0] = f"v_phi_{tag}"
            s.op("scalar", lambda sc: sc.activation(sb[:, 0:n], pt[:, 0:n], AF.Relu, scale=-1.0),
                 waits=w0, wp=f"s_mn_{tag}")
            s.op("scalar", lambda sc: sc.activation(sb[:, 512:512 + n], sb[:, 0:n], AF.Exp, scale=-1.0),
                 wp=f"s_ex_{tag}")
            s.op("vector", lambda v: v.scalar_tensor_tensor(dst_ap, pt[:, 0:n], 0.0,
                                                            sb[:, 512:512 + n], ALU.max, ALU.add),
                 waits=(f"s_ex_{tag}",), wp=f"v_phi_{tag}")

        def emit_kvmm(T):
            pkS = pk if T % 2 == 0 else hfm[:, E:2 * E]
            vbS = vbuf if T % 2 == 0 else hfm[:, 2 * E:3 * E]
            w0 = [f"v_phi_a{T}k0", f"v_phi_a{T}k1", f"v_vex_a{T}0", f"v_vex_a{T}1", "ld_ones16"]
            if T >= 1:
                w0.append(f"v_kvadd{T - 1}")
            for c in range(8):
                tns = KV1 if c < 4 else KV2
                col = P * (c % 4)
                s.op("tensor", lambda t, tns=tns, col=col, c=c:
                     t.matmul(tns[:, col: col + D],
                              pkS[:, P * c: P * (c + 1)],
                              vbS[:, D * 2 * c: D * 2 * c + D],
                              start=True, stop=True),
                     waits=tuple(w0) if c == 0 else ())
                s.op("tensor", lambda t, tns=tns, col=col, c=c:
                     t.matmul(tns[:, col + D: col + 2 * D],
                              pkS[:, P * c: P * (c + 1)],
                              vbS[:, D * (2 * c + 1): D * (2 * c + 1) + D],
                              start=True, stop=True))
                s.op("tensor", lambda t, c=c:
                     t.matmul(PDEN[:, c: c + 1],
                              pkS[:, P * c: P * (c + 1)],
                              ones16[:, 0:1],
                              start=True, stop=True),
                     wp=f"t_kvmm{T}" if c == 7 else None)
            for i, (srcp, acol, n) in enumerate([(KV1, 0, 512), (KV2, 512, 512), (PDEN, 1024, 8)]):
                s.op("vector", lambda v, srcp=srcp, acol=acol, n=n: v.tensor_tensor(
                    kvacc[:, acol: acol + n], kvacc[:, acol: acol + n],
                    srcp[:, 0:n], ALU.add),
                    waits=(f"t_kvmm{T}", "v_kvacc0") if i == 0 else (),
                    wp=f"v_kvadd{T}" if i == 2 else None,
                    nodrain=(i >= 1))

        dma("sync", idn[:, :], idn_h[:, :], wp="ld_idn")
        dma("sync", brows[0:1, 4 * E: 4 * E + 512], ones_h[:, :], wp="ld_ones")
        dma("sync", ones16[:, :], ones16_h[:, :], wp="ld_ones16")

        dma("sync", bmask[:, :], bmask_h[:, :], wp="ld_bmask")
        dma("gpsimd", brows[0:1, 0:3 * E], bqkv_h.ap().rearrange("(a n) -> a n", a=1), wp="ld_bq")
        dma("gpsimd", brows[0:1, 3 * E:4 * E], b2_h.ap().rearrange("(a n) -> a n", a=1), wp="ld_b2")
        dma("gpsimd", b1st[:, :], b1_h.ap().rearrange("(c p) -> c p", p=P), wp="ld_b1st")
        for e in range(EC):
            dma("gpsimd", WREG[:, WQ_OFF + 3072 * e + 1024: WQ_OFF + 3072 * (e + 1)],
                wqkv_h[P * e: P * (e + 1), 1024:3072], wp=f"ld_wkv{e}")
        for e in range(EC):
            dma("gpsimd", WREG[:, WQ_OFF + 3072 * e: WQ_OFF + 3072 * e + 1024],
                wqkv_h[P * e: P * (e + 1), 0:1024], wp=f"ld_wq{e}")

        s.op("tensor", lambda t: t.transpose(PT[:, 0:FC], b1st[:, :], idn[0:FC, 0:FC]),
             waits=("ld_b1st", "ld_idn"), wp="t_b1T")
        s.op("vector", lambda v: v.tensor_copy(b1pp[:, :], PT[:, 0:FC]),
             waits=("t_b1T",), wp="v_b1pp")

        s.op("vector", lambda v: v.memset(kvb[:, :], 0.0), wp="v_kvb0")
        s.op("vector", lambda v: v.memset(kvacc[:, :], 0.0), wp="v_kvacc0")
        s.op("vector", lambda v: v.memset(ksb[:, :], 0.0), wp="v_ksb0")

        def emit_ar_start(idx, lastT, xgate):
            cin = ccin_h if idx == 2 else ccin2_h
            cout = ccout_h if idx == 2 else ccout2_h
            for h in range(H):
                cp, half = h // 2, h % 2
                kv_src = kvacc[64 * half: 64 * half + 64, P * cp + D * half: P * cp + D * half + D]
                kv_dst = ccsb[64 * half: 64 * half + 64, 65 * cp: 65 * cp + D]
                w0 = (f"v_kvadd{lastT}",) if h == 0 else ()
                if h == 0 and idx == 2:
                    w0 = w0 + ("g_ccin1",)
                s.op("vector", lambda v, kv_src=kv_src, kv_dst=kv_dst: v.tensor_copy(kv_dst, kv_src),
                     waits=w0, wp=None, nodrain=(h >= 1))
                ks_src = kvacc[64 * half: 64 * half + 64, 1024 + cp: 1024 + cp + 1]
                ks_dst = ccsb[64 * half: 64 * half + 64, 65 * cp + D: 65 * cp + D + 1]
                s.op("vector", lambda v, ks_src=ks_src, ks_dst=ks_dst: v.tensor_copy(ks_dst, ks_src),
                     wp=f"v_cc{idx}_{h}", nodrain=True)
            s.op("vector", lambda v: v.memset(kvacc[:, :], 0.0),
                 wp=f"v_kvz{idx}")
            if idx == 1:
                dma("scalar", cin[:, :], ccsb[:, :], waits=(f"v_cc{idx}_{H - 1}",),
                    wp=f"g_ccin{idx}")
                s.op("gpsimd", lambda g, cin=cin, cout=cout: g.collective_compute(
                    "AllReduce", ALU.add, replica_groups=[[0, 1, 2, 3], [4, 5, 6, 7]],
                    ins=[cin.ap().opt()], outs=[cout.ap().opt()]),
                    waits=(f"g_ccin{idx}",), cc=True, wp=f"cc_done{idx}")
            else:
                ccfull = xa[:, 2 * E: 2 * E + 1040]
                s.op("vector", lambda v, ccfull=ccfull: v.tensor_scalar(
                    ccfull[:, 0:520], ccsb[:, :], bmask[:, 0:1], None, ALU.mult),
                    waits=(f"v_cc{idx}_{H - 1}", "ld_bmask") + tuple(xgate))
                s.op("vector", lambda v, ccfull=ccfull: v.tensor_scalar(
                    ccfull[:, 520:1040], ccsb[:, :], bmask[:, 1:2], None, ALU.mult),
                    wp=f"v_ccfull{idx}", nodrain=True)
                dma("scalar", cin[:, :], ccfull[:, :], waits=(f"v_ccfull{idx}",), wp=f"g_ccin{idx}")
                s.op("gpsimd", lambda g, cin=cin, cout=cout: g.collective_compute(
                    "AllReduce", ALU.add, replica_groups=[[0, 1, 2, 3, 4, 5, 6, 7]],
                    ins=[cin.ap().opt()], outs=[cout.ap().opt()]),
                    waits=(f"g_ccin{idx}",), cc=True, wp=f"cc_done{idx}")


        def abuf(T):
            return lntok if T % 2 == 0 else hfm[:, 3 * E:4 * E]

        def albuf(g):
            return lnf if g % 2 == 0 else hfm[:, 4 * E:8 * E]

        dma("sync", xa[:, 0:E], x_h[0:P, :], wp="ld_x_a0")
        emit_ln_v("a0", 0, ("ld_x_a0",), (), par=0)
        emit_ln_s("a0", 0, None, buf=abuf(0), par=0)

        def emit_q(g, after_q=False):
            LH, LB = (lnf, 0) if g % 2 == 0 else (hfm, 4 * E)
            lastT = min(AG * (g + 1) + AG - 1, NT - 1)
            for fc in range(EC):
                pt = [PA, PB, P3, P4][fc % 4]
                wts = [f"v_lnf_a{AG * g + AG - 1}"] + [f"ld_wq{e}" for e in range(EC)]
                if fc >= 4:
                    wts += [f"v_phi_q{g}_{fc - 4}", f"s_ex_q{g}_{fc - 4}"]
                elif after_q:
                    wts += [f"v_phi_q{g - 1}_{fc + 4}", f"s_ex_q{g - 1}_{fc + 4}"]
                elif fc < 2:
                    wts += [f"v_phi_a{lastT}k{fc}", f"s_ex_a{lastT}k{fc}"]
                else:
                    wts.append(f"v_vex_a{lastT}{fc - 2}")
                for e in range(EC):
                    s.op("tensor", lambda t, e=e, fc=fc, pt=pt, LH=LH, LB=LB:
                         t.matmul(pt[:, :],
                                  WREG[:, WQ_OFF + 3072 * e + P * fc: WQ_OFF + 3072 * e + P * (fc + 1)],
                                  LH[:, LB + AG * P * e: LB + AG * P * (e + 1)],
                                  start=(e == 0), stop=False),
                         waits=tuple(wts) if e == 0 else ())
                s.op("tensor", lambda t, fc=fc, pt=pt:
                     t.matmul(pt[:, :], brows[0:1, P * fc: P * (fc + 1)],
                              brows[0:1, 4 * E: 4 * E + 512],
                              start=False, stop=True),
                     wp=f"t_q{g}_{fc}")
                phw = [f"t_q{g}_{fc}"]
                if g >= 1 and fc == 0:
                    phw.append(f"sp_spill{g - 1}")
                emit_phi(f"q{g}_{fc}", pt, 512, pq[:, 512 * fc: 512 * (fc + 1)], phw)
            dma("scalar", pqsp_h[g, :, :], pq[:, :],
                waits=(f"v_phi_q{g}_{EC - 1}",), wp=f"sp_spill{g}")

        for g in range(NAG):
            LH, LB = (lnf, 0) if g % 2 == 0 else (hfm, 4 * E)
            for j in range(AG):
                T = AG * g + j
                slot = T % 4
                tag = f"a{T}"
                if T + 1 < NT:
                    Tn = T + 1
                    dma("sync", xa[:, E * (Tn % 4): E * (Tn % 4 + 1)], x_h[P * Tn: P * (Tn + 1), :],
                        waits=((f"s_ln_a{Tn - 4}",) if Tn >= 4 else ()), wp=f"ld_x_a{Tn}")
                    emit_ln_v(f"a{Tn}", Tn % 4, (f"ld_x_a{Tn}",), (), par=Tn % 2)

                emit_trans(tag, dst_off=P * j, dst_step=AG * P,
                           lnf_gate=(f"t_q{g - 2}_{EC - 1}" if (j == 0 and g >= 2) else None),
                           buf=abuf(T), dst=(LH, LB))

                banks = [PA, PB, P3, P4]
                pkT = pk if T % 2 == 0 else hfm[:, E:2 * E]
                vbT = vbuf if T % 2 == 0 else hfm[:, 2 * E:3 * E]
                kw = [f"v_lnf_{tag}", "ld_bq", "ld_ones"]
                vw0 = []
                if T >= 1:
                    vw0 += [f"v_vex_a{T - 1}0", f"v_vex_a{T - 1}1"]
                if j >= 1 or g == 0:
                    if T >= 1:
                        kw += [f"v_phi_a{T - 1}k0", f"v_phi_a{T - 1}k1", f"s_ex_a{T - 1}k0", f"s_ex_a{T - 1}k1"]
                else:
                    kw += [f"v_phi_q{g - 2}_4", f"v_phi_q{g - 2}_5"] if g >= 2 else                           [f"v_phi_a{T - 1}k0", f"v_phi_a{T - 1}k1"]
                    if g >= 2:
                        vw0 += [f"v_phi_q{g - 2}_6", f"v_phi_q{g - 2}_7", f"s_ex_q{g - 2}_6", f"s_ex_q{g - 2}_7"]
                for half in range(2):
                    ci0 = 2 * half
                    for e in range(EC):
                        for dci in range(2):
                            ci = ci0 + dci
                            col = 1024 + 512 * ci
                            w = []
                            if ci == ci0 and e == 0:
                                w = kw if half == 0 else vw0
                            if dci == 0:
                                w = list(w) + [f"ld_wkv{e}"]
                            s.op("tensor", lambda t, e=e, ci=ci, col=col, j=j, LH=LH, LB=LB:
                                 t.matmul(banks[ci][:, :],
                                          LH[:, LB + AG * P * e + P * j: LB + AG * P * e + P * (j + 1)],
                                          WREG[:, WQ_OFF + 3072 * e + col: WQ_OFF + 3072 * e + col + 512],
                                          start=(e == 0), stop=False),
                                 waits=tuple(w))
                    for dci in range(2):
                        ci = ci0 + dci
                        cn = ["k0", "k1", "v0", "v1"][ci]
                        col = 1024 + 512 * ci
                        s.op("tensor", lambda t, ci=ci, col=col:
                             t.matmul(banks[ci][:, :], brows[0:1, 4 * E: 4 * E + P],
                                      brows[0:1, col: col + 512], start=False, stop=True),
                             wp=f"t_{cn}_{tag}")
                    if half == 0:
                        for ci in (0, 1):
                            phw2 = [f"t_k{ci}_{tag}"]
                            if T >= 2:
                                phw2.append(f"t_kvmm{T - 2}")
                            emit_phi(f"{tag}k{ci}", banks[ci], 512,
                                     pkT[:, 512 * ci: 512 * (ci + 1)],
                                     tuple(phw2))
                        if T + 1 < NT:
                            emit_ln_s(f"a{T + 1}", (T + 1) % 4, f"t_tr_a{T - 1}" if T >= 1 else None,
                                      buf=abuf(T + 1), par=(T + 1) % 2)
                    else:
                        for ci in (2, 3):
                            vw = [f"t_v{ci - 2}_{tag}"]
                            if T >= 2:
                                vw.append(f"t_kvmm{T - 2}")
                            s.op("vector", lambda v, ci=ci, vbT=vbT: v.tensor_copy(
                                vbT[:, 512 * (ci - 2): 512 * (ci - 1)], banks[ci][:, :]),
                                waits=tuple(vw), wp=f"v_vex_{tag}{ci - 2}")
                if T >= 1:
                    emit_kvmm(T - 1)
                if T == 8:
                    emit_ar_start(1, 7, ())

            if g >= 1:
                emit_q(g - 1)
            if g == 0:
                for e in range(EC):
                    dma("gpsimd", WREG[:, W1_OFF + FF * e: W1_OFF + FF * (e + 1)],
                        w1_h[P * e: P * (e + 1), :], wp=f"ld_w1{e}")
            if g == 1:
                for q4 in range(2):
                    dst = WREG.ap().rearrange("p (c e) -> p c e", e=E)[
                        :, (W2_OFF // E) + 4 * q4: (W2_OFF // E) + 4 * (q4 + 1), :]
                    src2 = w2_h.ap().rearrange("(c p) e -> p c e", p=P)[:, 4 * q4: 4 * (q4 + 1), :]
                    dma("gpsimd", dst, src2, wp=f"ld_w2h{q4}")

        emit_kvmm(NT - 1)
        emit_ar_start(2, NT - 1, (f"s_ln_a{NT - 1}",))
        emit_q(NAG - 1, after_q=True)

        ccres = xa[:, 2 * E: 2 * E + 1040]
        dma("scalar", ccres[:, 0:520], ccout2_h[:, :], waits=("cc_done1", "g_ccin2"), wp="g_co1")
        s.op("vector", lambda v: v.tensor_tensor(
            ccsb[:, :], ccsb[:, :], ccres[:, 0:520], ALU.add),
            waits=("g_co1",), wp="v_sel1")
        dma("scalar", ccres, ccout_h[:, :], waits=("cc_done2", "v_sel1"), wp="g_co2")
        s.op("vector", lambda v: v.scalar_tensor_tensor(
            ccsb[:, :], ccres[:, 0:520], bmask[:, 0:1], ccsb[:, :], ALU.mult, ALU.add),
            waits=("g_co2",))
        s.op("vector", lambda v: v.scalar_tensor_tensor(
            ccsb[:, :], ccres[:, 520:1040], bmask[:, 1:2], ccsb[:, :], ALU.mult, ALU.add),
            wp="g_ccout")
        for q4 in range(6):
            dst = WREG.ap().rearrange("p (c e) -> p c e", e=E)[
                :, (W2_OFF // E) + 8 + 4 * q4: (W2_OFF // E) + 8 + 4 * (q4 + 1), :]
            src = w2_h.ap().rearrange("(c p) e -> p c e", p=P)[:, 8 + 4 * q4: 8 + 4 * (q4 + 1), :]
            dma("gpsimd", dst, src,
                waits=(f"t_q{NAG - 1}_{EC - 1}",) if q4 == 0 else (), wp=f"ld_w2t{q4}")

        for h in range(H):
            half, c2 = h % 2, h // 2
            s.op("vector", lambda v, half=half, c2=c2: v.tensor_copy(
                kvb[64 * half: 64 * half + 64, P * c2 + D * half: P * c2 + D * half + D],
                ccsb[64 * half: 64 * half + 64, 65 * c2: 65 * c2 + D]),
                waits=("g_ccout", "v_kvb0", "v_ksb0") if h == 0 else (), nodrain=(h >= 1))
            s.op("vector", lambda v, h=h, half=half, c2=c2: v.tensor_copy(
                ksb[64 * half: 64 * half + 64, H * c2 + h: H * c2 + h + 1],
                ccsb[64 * half: 64 * half + 64, 65 * c2 + D: 65 * c2 + D + 1]),
                wp="v_kvbdone" if h == H - 1 else None, nodrain=True)

        def emit_attn(bg):
            hb = bg % 2
            gsrc = bg // 2
            rw = [f"sp_spill{NAG - 1}"]
            if bg >= 2:
                rw.append(f"t_attn{2 * (bg - 2) + 1}")
            pq_dst = pq.ap()[:, 2048 * hb: 2048 * (hb + 1)].rearrange("p (c t) -> p c t", t=256)
            pq_src = pqsp_h.ap()[gsrc, :, :].rearrange("p (c t) -> p c t", t=AG * P)[
                :, :, 256 * (bg % 2): 256 * (bg % 2) + 256]
            dma("sync", pq_dst, pq_src, waits=tuple(rw), wp=f"ld_pq{bg}")

            for jj in range(BG):
                j = BG * bg + jj
                slot = j % 4
                tag = f"b{j}"
                gate = (f"sp_out{j - 4}",) if j >= 4 else (f"s_ln_a{12 + slot}",)
                if j < 4 and slot >= 2:
                    gate = gate + ("g_ccout",)
                dma("sync", xa[:, E * slot: E * (slot + 1)], x_h[P * j: P * (j + 1), :],
                    waits=gate, wp=f"ld_x_{tag}")

                aw = [f"ld_pq{bg}", "v_kvbdone"]
                if j >= 1:
                    aw.append(f"v_az_{j - 1}")
                else:
                    aw += [f"v_phi_q{NAG - 1}_{EC - 2}", f"v_phi_q{NAG - 1}_{EC - 1}",
                           f"s_ex_q{NAG - 1}_{EC - 2}", f"s_ex_q{NAG - 1}_{EC - 1}"]
                for c in range(EC):
                    pt = P3 if c < 4 else P4
                    s.op("tensor", lambda t, c=c, pt=pt, hb=hb, jj=jj:
                         t.matmul(pt[:, P * (c % 4): P * (c % 4 + 1)],
                                  pq[:, 2048 * hb + 256 * c + P * jj: 2048 * hb + 256 * c + P * (jj + 1)],
                                  kvb[:, P * c: P * (c + 1)], start=True, stop=True),
                         waits=tuple(aw) if c == 0 else ())
                    s.op("tensor", lambda t, c=c, hb=hb, jj=jj:
                         t.matmul(PDEN[:, :],
                                  pq[:, 2048 * hb + 256 * c + P * jj: 2048 * hb + 256 * c + P * (jj + 1)],
                                  ksb[:, H * c: H * (c + 1)],
                                  start=(c == 0), stop=(c == EC - 1)),
                         wp=(f"t_attn{j}" if c == EC - 1 else None))

                s.op("vector", lambda v: v.tensor_scalar(zbuf[:, 0:H], PDEN[:, :], 1e-6, None, ALU.add),
                     waits=(f"t_attn{j}", f"ld_x_{tag}"))
                s.op("vector", lambda v: v.reciprocal(zbuf[:, H:2 * H], zbuf[:, 0:H]), wp=f"v_z_{j}")
                for h in range(H):
                    pt = P3 if h < 8 else P4
                    s.op("vector", lambda v, h=h, pt=pt, slot=slot: v.scalar_tensor_tensor(
                        xa[:, E * slot + D * h: E * slot + D * (h + 1)],
                        pt[:, D * (h % 8): D * (h % 8 + 1)],
                        zbuf[:, H + h: H + h + 1],
                        xa[:, E * slot + D * h: E * slot + D * (h + 1)],
                        ALU.mult, ALU.add),
                        waits=(f"v_z_{j}",) if h == 0 else (),
                        wp=f"v_az_{j}" if h == H - 1 else None,
                        nodrain=(h >= 1))

        def emit_ln2(bg):
            for jj in range(BG):
                j = BG * bg + jj
                tag = f"b{j}"
                if j >= 2:
                    lg = f"t_tr_b{j - 2}"
                elif jj == 0:
                    lg = f"t_tr_a{NT - 1}"
                else:
                    lg = f"t_kvmm{NT - 1}"
                emit_ln(tag, j % 4, (f"v_az_{j}",), lg, (),
                        buf=(lntok if jj == 0 else pk), par=j % 2)

        def emit_trans2(bg):
            hb = bg % 2
            for jj in range(BG):
                emit_trans(f"b{BG * bg + jj}", dst_off=2048 * hb + P * jj, dst_step=BG * P,
                           lnf_gate=(f"t_f1{bg - 2}_{FC - 1}" if bg >= 2 else None),
                           buf=(lntok if jj == 0 else pk))

        def emit_ffn1(bg):
            hb = bg % 2
            for f in range(FC):
                pt = KV1 if f % 2 == 0 else KV2
                wts = [f"v_lnf_b{BG * bg + BG - 1}"] + [f"ld_w1{e}" for e in range(EC)]
                if bg == 0 and f < 2:
                    wts.append(f"v_kvadd{NT - 1}")
                if f >= 2:
                    wts.append(f"s_gel{bg}_{f - 2}")
                elif bg >= 1:
                    wts.append(f"s_gel{bg - 1}_{FC - 2 + f}")
                for e in range(EC):
                    s.op("tensor", lambda t, e=e, f=f, pt=pt, hb=hb:
                         t.matmul(pt[:, 0:256],
                                  WREG[:, W1_OFF + FF * e + P * f: W1_OFF + FF * e + P * (f + 1)],
                                  lnf[:, 2048 * hb + BG * P * e: 2048 * hb + BG * P * (e + 1)],
                                  start=(e == 0), stop=(e == EC - 1)),
                         waits=tuple(wts) if e == 0 else (),
                         wp=f"t_f1{bg}_{f}" if e == EC - 1 else None)
                gw = [f"t_f1{bg}_{f}", "v_b1pp"]
                if bg >= 1 and f == 0:
                    gw.append(f"t_f2{bg - 1}_{BG - 1}")
                s.op("scalar", lambda sc, f=f, pt=pt: sc.activation(
                    hfm[:, 256 * f: 256 * (f + 1)], pt[:, 0:256], AF.Gelu, bias=b1pp[:, f:f + 1]),
                    waits=tuple(gw), wp=f"s_gel{bg}_{f}")

        def emit_ffn2(bg):
            for jj in range(BG):
                j = BG * bg + jj
                wts = [f"s_gel{bg}_{FC - 1}", f"s_ln_b{j}"]
                if bg == 0:
                    wts += ["ld_w2h0", "ld_w2h1"]
                if jj == 0:
                    if bg >= 1:
                        wts += [f"f2evac{bg - 1}_1_0", f"f2evac{bg - 1}_1_1"]
                    else:
                        wts += [f"v_phi_q{NAG - 1}_{EC - 2}", f"v_phi_q{NAG - 1}_{EC - 1}",
                                f"s_ex_q{NAG - 1}_{EC - 2}", f"s_ex_q{NAG - 1}_{EC - 1}"]
                else:
                    wts += [f"f2evac{bg}_0_0", f"f2evac{bg}_0_1"]
                for f in range(FC):
                    fw = list(wts) if f == 0 else []
                    if bg == 0 and f >= 8 and (f - 8) % 4 == 0:
                        fw.append(f"ld_w2t{(f - 8) // 4}")
                    s.op("tensor", lambda t, f=f, jj=jj:
                         t.matmul(PA[:, :], hfm[:, 256 * f + P * jj: 256 * f + P * (jj + 1)],
                                  WREG[:, W2_OFF + E * f: W2_OFF + E * f + 512],
                                  start=(f == 0), stop=False),
                         waits=tuple(fw))
                    s.op("tensor", lambda t, f=f, jj=jj:
                         t.matmul(PB[:, :], hfm[:, 256 * f + P * jj: 256 * f + P * (jj + 1)],
                                  WREG[:, W2_OFF + E * f + 512: W2_OFF + E * f + 1024],
                                  start=(f == 0), stop=False))
                s.op("tensor", lambda t:
                     t.matmul(PA[:, :], brows[0:1, 4 * E: 4 * E + P],
                              brows[0:1, 3 * E: 3 * E + 512], start=False, stop=True))
                s.op("tensor", lambda t:
                     t.matmul(PB[:, :], brows[0:1, 4 * E: 4 * E + P],
                              brows[0:1, 3 * E + 512: 4 * E], start=False, stop=True),
                     wp=f"t_f2{bg}_{jj}")
                for ec in range(2):
                    def _f2ev(v, slot=j % 4, ec=ec):
                        xsl = xa[:, E * slot + 512 * ec: E * slot + 512 * (ec + 1)]
                        return v.tensor_tensor(xsl, xsl, (PA if ec == 0 else PB)[:, :], ALU.add)
                    s.op("vector", _f2ev, waits=(f"t_f2{bg}_{jj}",) if ec == 0 else (),
                         wp=f"f2evac{bg}_{jj}_{ec}", nodrain=(ec == 1))
                slot = j % 4
                dma("scalar", out_h[P * j: P * (j + 1), :], xa[:, E * slot: E * (slot + 1)],
                    waits=(f"f2evac{bg}_{jj}_0", f"f2evac{bg}_{jj}_1"), wp=f"sp_out{j}")

        for bg in range(NBG):
            emit_attn(bg)
            if bg >= 1:
                emit_ffn1(bg - 1)
            emit_ln2(bg)
            if bg >= 1:
                emit_ffn2(bg - 1)
            emit_trans2(bg)
        emit_ffn1(NBG - 1)
        emit_ffn2(NBG - 1)

        s.op("scalar", lambda sc: sc.nop(), waits=(f"sp_out{NT - 1}",))

        @block.sync
        def _(e):
            s.emit("sync", e, sems)

        @block.gpsimd
        def _(e):
            s.emit("gpsimd", e, sems)

        @block.tensor
        def _(e):
            s.emit("tensor", e, sems)

        @block.vector
        def _(e):
            s.emit("vector", e, sems)

        @block.scalar
        def _(e):
            s.emit("scalar", e, sems)

    return nc


def kernel(L0_attn_x, L0_attn_W_qkv, L0_attn_b_qkv, L0_attn_W_proj, L0_attn_b_proj,
           L0_ffn_w1, L0_ffn_b1, L0_ffn_w2, L0_ffn_b2):
    x = np.ascontiguousarray(np.asarray(L0_attn_x, dtype=np.float32))
    wqkv = np.ascontiguousarray(np.asarray(L0_attn_W_qkv, dtype=np.float32))
    bqkv = np.ascontiguousarray(np.asarray(L0_attn_b_qkv, dtype=np.float32))
    w1 = np.ascontiguousarray(np.asarray(L0_ffn_w1, dtype=np.float32))
    b1 = np.ascontiguousarray(np.asarray(L0_ffn_b1, dtype=np.float32))
    w2 = np.ascontiguousarray(np.asarray(L0_ffn_w2, dtype=np.float32))
    b2 = np.ascontiguousarray(np.asarray(L0_ffn_b2, dtype=np.float32))

    nc = build_nc()
    in_maps = []
    for core in range(8):
        b = core // 4
        t0 = TOK * (core % 4)
        bm = np.zeros((128, 2), dtype=np.float32)
        bm[:, b] = 1.0
        in_maps.append({
            "x": np.ascontiguousarray(x[b, t0:t0 + TOK, :]),
            "wqkv": wqkv, "bqkv": bqkv, "w1": w1, "b1": b1, "w2": w2, "b2": b2,
            "bmask": bm,
        })
    res = run_bass_kernel_spmd(nc, in_maps, core_ids=list(range(8)))
    out = np.empty((2, 8192, 1024), dtype=np.float32)
    for core in range(8):
        b = core // 4
        t0 = TOK * (core % 4)
        out[b, t0:t0 + TOK, :] = res.results[core]["out"]
    return out


# version 15
# speedup vs baseline: 1.0493x; 1.0388x over previous
import numpy as np
import ml_dtypes
import concourse.bass as bass
import concourse.mybir as mybir
from concourse.bass_utils import run_bass_kernel_spmd

F32 = mybir.dt.float32
BF16 = mybir.dt.bfloat16
AF = mybir.ActivationFunctionType
ALU = mybir.AluOpType
AX = mybir.AxisListType

P = 128
TOK = 2048
NT = TOK // P
E = 1024
EC = E // P
H = 16
D = 64
FF = 4096
FC = FF // P
AG = 4
BG = 2
NAG = NT // AG
NBG = NT // BG

W1_OFF = 0
W2_OFF = 32768
WQ_OFF = 40960
WREG_COLS = 65536
VEXW = H * (D + 1)


class Sched:

    NROT = 6
    ENGQ = {"sync": "LX", "gpsimd": "LW", "scalar": "SP", "vector": "LV"}
    ESEM = {"tensor": "T", "vector": "V", "scalar": "S", "gpsimd": "G"}

    def __init__(self):
        self.ops = []
        self.cnt = {}
        self.wp = {}
        self.qidx = {}

    def _bump(self, sem, amt):
        self.cnt[sem] = self.cnt.get(sem, 0) + amt
        return self.cnt[sem]

    def op(self, engine, fn, waits=(), wp=None, dma=False, cc=False, nodrain=False):
        wl = []
        for w in waits:
            if w is None:
                continue
            assert w in self.wp, f"waypoint {w!r} not defined yet"
            wl.append(self.wp[w])
        inc = None
        if dma:
            q = self.ENGQ[engine]
            i = self.qidx.get(q, 0)
            self.qidx[q] = i + 1
            sem = f"{q}{i % self.NROT}"
            prev = self.cnt.get(sem, 0)
            if prev:
                wl.append((sem, prev))
            v = self._bump(sem, 16)
            inc = (sem, 16)
            if wp:
                self.wp[wp] = (sem, v)
        elif cc:
            v = self._bump("CC", 1)
            inc = ("CC", 1)
            if wp:
                self.wp[wp] = ("CC", v)
        elif wp:
            sem = self.ESEM[engine]
            v = self._bump(sem, 1)
            inc = (sem, 1)
            self.wp[wp] = (sem, v)
        drain = (not dma) and (not cc) and (not nodrain) and engine in ("vector", "scalar")
        self.ops.append((engine, fn, wl, inc, drain))

    def emit(self, eng_name, eng, sems):
        last = {}
        for engine, fn, waits, inc, drain in self.ops:
            if engine != eng_name:
                continue
            for sem, val in waits:
                if last.get(sem, 0) >= val:
                    continue
                eng.wait_ge(sems[sem], val)
                last[sem] = val
            if drain:
                eng.drain()
            instr = fn(eng)
            if inc is not None:
                assert instr is not None
                instr.then_inc(sems[inc[0]], inc[1])


def build_nc():
    nc = bass.Bass("TRN2", target_bir_lowering=False)

    x_h = nc.dram_tensor("x", [TOK, E], F32, kind="ExternalInput")
    wqkv_h = nc.dram_tensor("wqkv", [E, 3 * E], F32, kind="ExternalInput")
    bqkv_h = nc.dram_tensor("bqkv", [3 * E], F32, kind="ExternalInput")
    w1_h = nc.dram_tensor("w1", [E, FF], F32, kind="ExternalInput")
    b1_h = nc.dram_tensor("b1", [FF], F32, kind="ExternalInput")
    w2_h = nc.dram_tensor("w2", [FF, E], F32, kind="ExternalInput")
    b2_h = nc.dram_tensor("b2", [E], F32, kind="ExternalInput")
    out_h = nc.dram_tensor("out", [TOK, E], F32, kind="ExternalOutput")

    pqsp_h = nc.dram_tensor("pqspill", [NAG, P, AG * P * EC], BF16)
    bmask_h = nc.dram_tensor("bmask", [P, 2], F32, kind="ExternalInput")
    ccin_h = nc.dram_tensor("cc_in", [P, 1040], F32)
    ccout_h = nc.dram_tensor("cc_out", [P, 1040], F32)

    idn_h = nc.inline_tensor(np.eye(P, dtype=ml_dtypes.bfloat16), name="idnc")
    ones_h = nc.inline_tensor(np.ones((1, 512), dtype=ml_dtypes.bfloat16), name="onesc")
    ones16_h = nc.inline_tensor(np.ones((P, H), dtype=ml_dtypes.bfloat16), name="ones16c")

    from contextlib import ExitStack
    with ExitStack() as stack:
        en = stack.enter_context
        WREG = en(nc.sbuf_tensor("WREG", [P, WREG_COLS], BF16))
        pq = en(nc.sbuf_tensor("pq", [P, 4096], BF16))
        lnf = en(nc.sbuf_tensor("lnf", [P, 4096], BF16))
        lntok = en(nc.sbuf_tensor("lntok", [P, E], BF16))
        xa = en(nc.sbuf_tensor("xa", [P, 4 * E], F32))
        S23 = en(nc.sbuf_tensor("S23", [P, E], F32))
        pk = en(nc.sbuf_tensor("pk", [P, E], BF16))
        vbuf = en(nc.sbuf_tensor("vbuf", [P, E], BF16))
        hfm = en(nc.sbuf_tensor("hfm", [P, FC * BG * P], BF16))
        b1pp = en(nc.sbuf_tensor("b1pp", [P, FC], F32))
        b1st = en(nc.sbuf_tensor("b1st", [FC, P], BF16))
        brows = en(nc.sbuf_tensor("brows", [1, 4 * E + 512], BF16))
        ones16 = en(nc.sbuf_tensor("ones16", [P, H], BF16))
        ccsb = en(nc.sbuf_tensor("ccsb", [P, 520], F32))
        bmask = en(nc.sbuf_tensor("bmsk", [P, 2], F32))
        kvacc = en(nc.sbuf_tensor("kvacc", [P, 1032], F32))
        kvb = en(nc.sbuf_tensor("kvb", [P, E], BF16))
        ksb = en(nc.sbuf_tensor("ksb", [P, P], BF16))
        idn = en(nc.sbuf_tensor("idn", [P, P], BF16))
        stats = en(nc.sbuf_tensor("stats", [P, 16], F32))
        zbuf = en(nc.sbuf_tensor("zbuf", [P, 2 * H], F32))
        KV1 = en(nc.psum_tensor("KV1", [P, 512], F32))
        KV2 = en(nc.psum_tensor("KV2", [P, 512], F32))
        P3 = en(nc.psum_tensor("P3", [P, 512], F32))
        P4 = en(nc.psum_tensor("P4", [P, 512], F32))
        PT = en(nc.psum_tensor("PT", [P, E], BF16))
        PA = en(nc.psum_tensor("PA", [P, 512], F32))
        PB = en(nc.psum_tensor("PB", [P, 512], F32))
        PDEN = en(nc.psum_tensor("PDEN", [P, H], F32))
        sems = {}
        for q in ("LX", "LW", "SP", "LV"):
            for i in range(Sched.NROT):
                sems[f"{q}{i}"] = en(nc.semaphore(f"{q}{i}"))
        for nm in ("CC", "T", "V", "S", "G"):
            sems[nm] = en(nc.semaphore(nm))
        block = en(nc.Block())
        kvt = [KV1, KV2, P3, P4]
        s = Sched()

        def dma(engine, out, in_, waits=(), wp=None, **kw):
            s.op(engine, lambda e, o=out, i=in_, kw=kw: e.dma_start(out=o, in_=i, **kw),
                 waits=waits, wp=wp, dma=True)

        prev_ln = [None, None]

        def emit_ln_v(tag, slot, data_gates, s23_gates, par=0):
            so = 8 * par
            xs = xa[:, E * slot: E * (slot + 1)]
            g0 = tuple(data_gates) + ((f"s_ln_{prev_ln[par]}",) if prev_ln[par] else ())
            prev_ln[par] = tag
            s.op("vector", lambda v: v.reduce_sum(stats[:, so + 0:so + 1], xs, axis=AX.X),
                 waits=g0, wp=f"v_sum_{tag}")
            s.op("vector", lambda v: v.tensor_scalar(stats[:, so + 2:so + 3], stats[:, so + 0:so + 1], 1.0 / E, None, ALU.mult))
            s.op("vector", lambda v: v.tensor_scalar(stats[:, so + 3:so + 4], stats[:, so + 0:so + 1], -1.0 / E, None, ALU.mult),
                 wp=f"v_means_{tag}")
            sqdst = hfm[:, 0:E] if tag.startswith("a") else S23[:, :]
            s.op("scalar", lambda sc, sqdst=sqdst: sc.activation(sqdst, xs, AF.Square, accum_out=stats[:, so + 1:so + 2]),
                 waits=tuple(data_gates) + tuple(s23_gates), wp=f"s_sq_{tag}")
            s.op("vector", lambda v: v.tensor_scalar(stats[:, so + 4:so + 5], stats[:, so + 1:so + 2], 1.0 / E, None, ALU.mult),
                 waits=(f"s_sq_{tag}",))
            s.op("vector", lambda v: v.tensor_tensor(stats[:, so + 5:so + 6], stats[:, so + 2:so + 3], stats[:, so + 2:so + 3], ALU.mult))
            s.op("vector", lambda v: v.tensor_tensor(stats[:, so + 4:so + 5], stats[:, so + 4:so + 5], stats[:, so + 5:so + 6], ALU.subtract))
            s.op("vector", lambda v: v.tensor_scalar(stats[:, so + 4:so + 5], stats[:, so + 4:so + 5], 1e-5, None, ALU.add),
                 wp=f"v_rcp_{tag}")

        def emit_ln_s(tag, slot, lntok_gate, buf=None, par=0):
            buf = lntok if buf is None else buf
            so = 8 * par
            xs = xa[:, E * slot: E * (slot + 1)]
            s.op("scalar", lambda sc: sc.activation(stats[:, so + 5:so + 6], stats[:, so + 4:so + 5], AF.Ln),
                 waits=(f"v_rcp_{tag}",))
            s.op("scalar", lambda sc: sc.activation(stats[:, so + 6:so + 7], stats[:, so + 5:so + 6], AF.Exp, scale=-0.5),
                 wp=f"s_istd_{tag}")
            s.op("scalar", lambda sc: sc.activation(stats[:, so + 7:so + 8], stats[:, so + 3:so + 4], AF.Identity,
                                                    scale=stats[:, so + 6:so + 7]),
                 waits=(f"s_istd_{tag}", f"v_means_{tag}"), wp=f"s_nbm_{tag}")
            s.op("scalar", lambda sc, buf=buf: sc.activation(buf[:, :], xs, AF.Identity,
                                                    scale=stats[:, so + 6:so + 7], bias=stats[:, so + 7:so + 8]),
                 waits=(f"s_nbm_{tag}",) + ((lntok_gate,) if lntok_gate else ()),
                 wp=f"s_ln_{tag}")

        def emit_ln(tag, slot, data_gates, lntok_gate, s23_gates, buf=None, par=0):
            emit_ln_v(tag, slot, data_gates, s23_gates, par=par)
            emit_ln_s(tag, slot, lntok_gate, buf=buf, par=par)

        prev_trans = [None]

        def emit_trans(tag, dst_off, dst_step, lnf_gate=None, buf=None, dst=None):
            buf = lntok if buf is None else buf
            dh, db = (lnf, 0) if dst is None else dst
            gates0 = [f"s_ln_{tag}", "ld_idn"]
            if prev_trans[0]:
                gates0.append(f"v_lnf_{prev_trans[0]}")
            else:
                gates0.append("v_b1pp")
            for c in range(EC):
                s.op("tensor", lambda t, c=c, buf=buf: t.transpose(PT[:, P * c: P * (c + 1)],
                                                          buf[:, P * c: P * (c + 1)], idn[:, :]),
                     waits=tuple(gates0) if c == 0 else (),
                     wp=(f"t_tr_{tag}" if c == EC - 1 else None))

            def _evac(v, dst_off=dst_off, dst_step=dst_step, dh=dh, db=db):
                dstap = bass.AP(dh, db + dst_off, [[dh.shape[1], P], [dst_step, EC], [1, P]])
                src = PT.ap().rearrange("p (c t) -> p c t", t=P)
                return v.tensor_copy(dstap, src)
            s.op("vector", _evac, waits=(f"t_tr_{tag}",) + ((lnf_gate,) if lnf_gate else ()),
                 wp=f"v_lnf_{tag}")
            prev_trans[0] = tag

        prev_phi = [None]

        def emit_phi(tag, pt, n, dst_ap, waits, scr=None):
            sb = scr if scr is not None else S23
            w0 = tuple(waits) + ((prev_phi[0],) if prev_phi[0] else ())
            prev_phi[0] = f"v_phi_{tag}"
            s.op("scalar", lambda sc: sc.activation(sb[:, 0:n], pt[:, 0:n], AF.Relu, scale=-1.0),
                 waits=w0, wp=f"s_mn_{tag}")
            s.op("scalar", lambda sc: sc.activation(sb[:, 512:512 + n], sb[:, 0:n], AF.Exp, scale=-1.0),
                 wp=f"s_ex_{tag}")
            s.op("vector", lambda v: v.scalar_tensor_tensor(dst_ap, pt[:, 0:n], 0.0,
                                                            sb[:, 512:512 + n], ALU.max, ALU.add),
                 waits=(f"s_ex_{tag}",), wp=f"v_phi_{tag}")

        def emit_kvmm(T):
            pkS = pk if T % 2 == 0 else hfm[:, E:2 * E]
            vbS = vbuf if T % 2 == 0 else hfm[:, 2 * E:3 * E]
            w0 = [f"v_phi_a{T}k0", f"v_phi_a{T}k1", f"v_vex_a{T}0", f"v_vex_a{T}1", "ld_ones16"]
            if T >= 1:
                w0.append(f"v_kvadd{T - 1}")
            for c in range(8):
                tns = KV1 if c < 4 else KV2
                col = P * (c % 4)
                s.op("tensor", lambda t, tns=tns, col=col, c=c:
                     t.matmul(tns[:, col: col + D],
                              pkS[:, P * c: P * (c + 1)],
                              vbS[:, D * 2 * c: D * 2 * c + D],
                              start=True, stop=True),
                     waits=tuple(w0) if c == 0 else ())
                s.op("tensor", lambda t, tns=tns, col=col, c=c:
                     t.matmul(tns[:, col + D: col + 2 * D],
                              pkS[:, P * c: P * (c + 1)],
                              vbS[:, D * (2 * c + 1): D * (2 * c + 1) + D],
                              start=True, stop=True))
                s.op("tensor", lambda t, c=c:
                     t.matmul(PDEN[:, c: c + 1],
                              pkS[:, P * c: P * (c + 1)],
                              ones16[:, 0:1],
                              start=True, stop=True),
                     wp=f"t_kvmm{T}" if c == 7 else None)
            for i, (srcp, acol, n) in enumerate([(KV1, 0, 512), (KV2, 512, 512), (PDEN, 1024, 8)]):
                s.op("vector", lambda v, srcp=srcp, acol=acol, n=n: v.tensor_tensor(
                    kvacc[:, acol: acol + n], kvacc[:, acol: acol + n],
                    srcp[:, 0:n], ALU.add),
                    waits=(f"t_kvmm{T}", "v_kvacc0") if i == 0 else (),
                    wp=f"v_kvadd{T}" if i == 2 else None,
                    nodrain=(i >= 1))

        dma("sync", idn[:, :], idn_h[:, :], wp="ld_idn")
        dma("sync", brows[0:1, 4 * E: 4 * E + 512], ones_h[:, :], wp="ld_ones")
        dma("sync", ones16[:, :], ones16_h[:, :], wp="ld_ones16")

        dma("sync", bmask[:, :], bmask_h[:, :], wp="ld_bmask")
        dma("gpsimd", brows[0:1, 0:3 * E], bqkv_h.ap().rearrange("(a n) -> a n", a=1), wp="ld_bq")
        dma("gpsimd", brows[0:1, 3 * E:4 * E], b2_h.ap().rearrange("(a n) -> a n", a=1), wp="ld_b2")
        dma("gpsimd", b1st[:, :], b1_h.ap().rearrange("(c p) -> c p", p=P), wp="ld_b1st")
        for e in range(EC):
            dma("gpsimd", WREG[:, WQ_OFF + 3072 * e + 1024: WQ_OFF + 3072 * (e + 1)],
                wqkv_h[P * e: P * (e + 1), 1024:3072], wp=f"ld_wkv{e}")
        for e in range(EC):
            dma("gpsimd", WREG[:, WQ_OFF + 3072 * e: WQ_OFF + 3072 * e + 1024],
                wqkv_h[P * e: P * (e + 1), 0:1024], wp=f"ld_wq{e}")

        s.op("tensor", lambda t: t.transpose(PT[:, 0:FC], b1st[:, :], idn[0:FC, 0:FC]),
             waits=("ld_b1st", "ld_idn"), wp="t_b1T")
        s.op("vector", lambda v: v.tensor_copy(b1pp[:, :], PT[:, 0:FC]),
             waits=("t_b1T",), wp="v_b1pp")

        s.op("vector", lambda v: v.memset(kvb[:, :], 0.0), wp="v_kvb0")
        s.op("vector", lambda v: v.memset(kvacc[:, :], 0.0), wp="v_kvacc0")
        s.op("vector", lambda v: v.memset(ksb[:, :], 0.0), wp="v_ksb0")

        def emit_ar_start():
            for h in range(H):
                cp, half = h // 2, h % 2
                kv_src = kvacc[64 * half: 64 * half + 64, P * cp + D * half: P * cp + D * half + D]
                kv_dst = ccsb[64 * half: 64 * half + 64, 65 * cp: 65 * cp + D]
                s.op("vector", lambda v, kv_src=kv_src, kv_dst=kv_dst: v.tensor_copy(kv_dst, kv_src),
                     waits=(f"v_kvadd{NT - 1}",) if h == 0 else (),
                     wp=None, nodrain=(h >= 1))
                ks_src = kvacc[64 * half: 64 * half + 64, 1024 + cp: 1024 + cp + 1]
                ks_dst = ccsb[64 * half: 64 * half + 64, 65 * cp + D: 65 * cp + D + 1]
                s.op("vector", lambda v, ks_src=ks_src, ks_dst=ks_dst: v.tensor_copy(ks_dst, ks_src),
                     wp=f"v_cc{h}", nodrain=True)
            ccfull = xa[:, 2 * E: 2 * E + 1040]
            s.op("vector", lambda v, ccfull=ccfull: v.tensor_scalar(
                ccfull[:, 0:520], ccsb[:, :], bmask[:, 0:1], None, ALU.mult),
                waits=(f"v_cc{H - 1}", "ld_bmask", f"s_ln_a{NT - 1}"))
            s.op("vector", lambda v, ccfull=ccfull: v.tensor_scalar(
                ccfull[:, 520:1040], ccsb[:, :], bmask[:, 1:2], None, ALU.mult),
                wp="v_ccfull", nodrain=True)
            dma("scalar", ccin_h[:, :], ccfull[:, :], waits=("v_ccfull",), wp="g_ccin")
            s.op("gpsimd", lambda g: g.collective_compute(
                "AllReduce", ALU.add, replica_groups=[[0, 1, 2, 3, 4, 5, 6, 7]],
                ins=[ccin_h.ap().opt()], outs=[ccout_h.ap().opt()]),
                waits=("g_ccin",), cc=True, wp="cc_done")


        def abuf(T):
            return lntok if T % 2 == 0 else hfm[:, 3 * E:4 * E]

        def albuf(g):
            return lnf if g % 2 == 0 else hfm[:, 4 * E:8 * E]

        dma("sync", xa[:, 0:E], x_h[0:P, :], wp="ld_x_a0")
        emit_ln_v("a0", 0, ("ld_x_a0",), (), par=0)
        emit_ln_s("a0", 0, None, buf=abuf(0), par=0)

        def emit_q(g, after_q=False):
            LH, LB = (lnf, 0) if g % 2 == 0 else (hfm, 4 * E)
            lastT = min(AG * (g + 1) + AG - 1, NT - 1)
            for fc in range(EC):
                pt = [PA, PB, P3, P4][fc % 4]
                wts = [f"v_lnf_a{AG * g + AG - 1}"] + [f"ld_wq{e}" for e in range(EC)]
                if fc >= 4:
                    wts += [f"v_phi_q{g}_{fc - 4}", f"s_ex_q{g}_{fc - 4}"]
                elif after_q:
                    wts += [f"v_phi_q{g - 1}_{fc + 4}", f"s_ex_q{g - 1}_{fc + 4}"]
                elif fc < 2:
                    wts += [f"v_phi_a{lastT}k{fc}", f"s_ex_a{lastT}k{fc}"]
                else:
                    wts.append(f"v_vex_a{lastT}{fc - 2}")
                for e in range(EC):
                    s.op("tensor", lambda t, e=e, fc=fc, pt=pt, LH=LH, LB=LB:
                         t.matmul(pt[:, :],
                                  WREG[:, WQ_OFF + 3072 * e + P * fc: WQ_OFF + 3072 * e + P * (fc + 1)],
                                  LH[:, LB + AG * P * e: LB + AG * P * (e + 1)],
                                  start=(e == 0), stop=False),
                         waits=tuple(wts) if e == 0 else ())
                s.op("tensor", lambda t, fc=fc, pt=pt:
                     t.matmul(pt[:, :], brows[0:1, P * fc: P * (fc + 1)],
                              brows[0:1, 4 * E: 4 * E + 512],
                              start=False, stop=True),
                     wp=f"t_q{g}_{fc}")
                phw = [f"t_q{g}_{fc}"]
                if g >= 1 and fc == 0:
                    phw.append(f"sp_spill{g - 1}")
                emit_phi(f"q{g}_{fc}", pt, 512, pq[:, 512 * fc: 512 * (fc + 1)], phw)
            dma("scalar", pqsp_h[g, :, :], pq[:, :],
                waits=(f"v_phi_q{g}_{EC - 1}",), wp=f"sp_spill{g}")

        for g in range(NAG):
            LH, LB = (lnf, 0) if g % 2 == 0 else (hfm, 4 * E)
            for j in range(AG):
                T = AG * g + j
                slot = T % 4
                tag = f"a{T}"
                if T + 1 < NT:
                    Tn = T + 1
                    dma("sync", xa[:, E * (Tn % 4): E * (Tn % 4 + 1)], x_h[P * Tn: P * (Tn + 1), :],
                        waits=((f"s_ln_a{Tn - 4}",) if Tn >= 4 else ()), wp=f"ld_x_a{Tn}")

                emit_trans(tag, dst_off=P * j, dst_step=AG * P,
                           lnf_gate=(f"t_q{g - 2}_{EC - 1}" if (j == 0 and g >= 2) else None),
                           buf=abuf(T), dst=(LH, LB))

                if T + 1 < NT:
                    emit_ln_v(f"a{T + 1}", (T + 1) % 4, (f"ld_x_a{T + 1}",), (), par=(T + 1) % 2)

                banks = [PA, PB, P3, P4]
                pkT = pk if T % 2 == 0 else hfm[:, E:2 * E]
                vbT = vbuf if T % 2 == 0 else hfm[:, 2 * E:3 * E]
                kw = [f"v_lnf_{tag}", "ld_bq", "ld_ones"]
                vw0 = []
                if T >= 1:
                    vw0 += [f"v_vex_a{T - 1}0", f"v_vex_a{T - 1}1"]
                if j >= 1 or g == 0:
                    if T >= 1:
                        kw += [f"v_phi_a{T - 1}k0", f"v_phi_a{T - 1}k1", f"s_ex_a{T - 1}k0", f"s_ex_a{T - 1}k1"]
                else:
                    kw += [f"v_phi_q{g - 2}_4", f"v_phi_q{g - 2}_5"] if g >= 2 else                           [f"v_phi_a{T - 1}k0", f"v_phi_a{T - 1}k1"]
                    if g >= 2:
                        vw0 += [f"v_phi_q{g - 2}_6", f"v_phi_q{g - 2}_7", f"s_ex_q{g - 2}_6", f"s_ex_q{g - 2}_7"]
                for half in range(2):
                    ci0 = 2 * half
                    for e in range(EC):
                        for dci in range(2):
                            ci = ci0 + dci
                            col = 1024 + 512 * ci
                            w = []
                            if ci == ci0 and e == 0:
                                w = kw if half == 0 else vw0
                            if dci == 0:
                                w = list(w) + [f"ld_wkv{e}"]
                            s.op("tensor", lambda t, e=e, ci=ci, col=col, j=j, LH=LH, LB=LB:
                                 t.matmul(banks[ci][:, :],
                                          LH[:, LB + AG * P * e + P * j: LB + AG * P * e + P * (j + 1)],
                                          WREG[:, WQ_OFF + 3072 * e + col: WQ_OFF + 3072 * e + col + 512],
                                          start=(e == 0), stop=False),
                                 waits=tuple(w))
                    for dci in range(2):
                        ci = ci0 + dci
                        cn = ["k0", "k1", "v0", "v1"][ci]
                        col = 1024 + 512 * ci
                        s.op("tensor", lambda t, ci=ci, col=col:
                             t.matmul(banks[ci][:, :], brows[0:1, 4 * E: 4 * E + P],
                                      brows[0:1, col: col + 512], start=False, stop=True),
                             wp=f"t_{cn}_{tag}")
                    if half == 0:
                        for ci in (0, 1):
                            phw2 = [f"t_k{ci}_{tag}"]
                            if T >= 2:
                                phw2.append(f"t_kvmm{T - 2}")
                            emit_phi(f"{tag}k{ci}", banks[ci], 512,
                                     pkT[:, 512 * ci: 512 * (ci + 1)],
                                     tuple(phw2))
                        if T + 1 < NT:
                            emit_ln_s(f"a{T + 1}", (T + 1) % 4, f"t_tr_a{T - 1}" if T >= 1 else None,
                                      buf=abuf(T + 1), par=(T + 1) % 2)
                    else:
                        for ci in (2, 3):
                            vw = [f"t_v{ci - 2}_{tag}"]
                            if T >= 2:
                                vw.append(f"t_kvmm{T - 2}")
                            s.op("vector", lambda v, ci=ci, vbT=vbT: v.tensor_copy(
                                vbT[:, 512 * (ci - 2): 512 * (ci - 1)], banks[ci][:, :]),
                                waits=tuple(vw), wp=f"v_vex_{tag}{ci - 2}")
                if T >= 1:
                    emit_kvmm(T - 1)

            if g >= 1:
                emit_q(g - 1)
            if g == 0:
                for e in range(EC):
                    dma("gpsimd", WREG[:, W1_OFF + FF * e: W1_OFF + FF * (e + 1)],
                        w1_h[P * e: P * (e + 1), :], wp=f"ld_w1{e}")
            if g == 1:
                for q4 in range(2):
                    dst = WREG.ap().rearrange("p (c e) -> p c e", e=E)[
                        :, (W2_OFF // E) + 4 * q4: (W2_OFF // E) + 4 * (q4 + 1), :]
                    src2 = w2_h.ap().rearrange("(c p) e -> p c e", p=P)[:, 4 * q4: 4 * (q4 + 1), :]
                    dma("gpsimd", dst, src2, wp=f"ld_w2h{q4}")

        emit_kvmm(NT - 1)
        emit_ar_start()
        emit_q(NAG - 1, after_q=True)

        ccfull2 = xa[:, 2 * E: 2 * E + 1040]
        dma("scalar", ccfull2, ccout_h[:, :], waits=("cc_done",), wp="g_ccout_pre")
        s.op("vector", lambda v: v.tensor_scalar(
            ccsb[:, :], ccfull2[:, 0:520], bmask[:, 0:1], None, ALU.mult),
            waits=("g_ccout_pre",))
        s.op("vector", lambda v: v.scalar_tensor_tensor(
            ccsb[:, :], ccfull2[:, 520:1040], bmask[:, 1:2], ccsb[:, :], ALU.mult, ALU.add),
            wp="g_ccout")
        for q4 in range(6):
            dst = WREG.ap().rearrange("p (c e) -> p c e", e=E)[
                :, (W2_OFF // E) + 8 + 4 * q4: (W2_OFF // E) + 8 + 4 * (q4 + 1), :]
            src = w2_h.ap().rearrange("(c p) e -> p c e", p=P)[:, 8 + 4 * q4: 8 + 4 * (q4 + 1), :]
            dma("gpsimd", dst, src,
                waits=(f"t_q{NAG - 1}_{EC - 1}",) if q4 == 0 else (), wp=f"ld_w2t{q4}")

        for h in range(H):
            half, c2 = h % 2, h // 2
            s.op("vector", lambda v, half=half, c2=c2: v.tensor_copy(
                kvb[64 * half: 64 * half + 64, P * c2 + D * half: P * c2 + D * half + D],
                ccsb[64 * half: 64 * half + 64, 65 * c2: 65 * c2 + D]),
                waits=("g_ccout", "v_kvb0", "v_ksb0") if h == 0 else (), nodrain=(h >= 1))
            s.op("vector", lambda v, h=h, half=half, c2=c2: v.tensor_copy(
                ksb[64 * half: 64 * half + 64, H * c2 + h: H * c2 + h + 1],
                ccsb[64 * half: 64 * half + 64, 65 * c2 + D: 65 * c2 + D + 1]),
                wp="v_kvbdone" if h == H - 1 else None, nodrain=True)

        def emit_attn(bg):
            hb = bg % 2
            gsrc = bg // 2
            rw = [f"sp_spill{NAG - 1}"]
            if bg >= 2:
                rw.append(f"t_attn{2 * (bg - 2) + 1}")
            pq_dst = pq.ap()[:, 2048 * hb: 2048 * (hb + 1)].rearrange("p (c t) -> p c t", t=256)
            pq_src = pqsp_h.ap()[gsrc, :, :].rearrange("p (c t) -> p c t", t=AG * P)[
                :, :, 256 * (bg % 2): 256 * (bg % 2) + 256]
            dma("sync", pq_dst, pq_src, waits=tuple(rw), wp=f"ld_pq{bg}")

            for jj in range(BG):
                j = BG * bg + jj
                slot = j % 4
                tag = f"b{j}"
                gate = (f"sp_out{j - 4}",) if j >= 4 else (f"s_ln_a{12 + slot}",)
                if j < 4 and slot >= 2:
                    gate = gate + ("g_ccout",)
                dma("sync", xa[:, E * slot: E * (slot + 1)], x_h[P * j: P * (j + 1), :],
                    waits=gate, wp=f"ld_x_{tag}")

                aw = [f"ld_pq{bg}", "v_kvbdone"]
                if j >= 1:
                    aw.append(f"v_az_{j - 1}")
                else:
                    aw += [f"v_phi_q{NAG - 1}_{EC - 2}", f"v_phi_q{NAG - 1}_{EC - 1}",
                           f"s_ex_q{NAG - 1}_{EC - 2}", f"s_ex_q{NAG - 1}_{EC - 1}"]
                for c in range(EC):
                    pt = P3 if c < 4 else P4
                    s.op("tensor", lambda t, c=c, pt=pt, hb=hb, jj=jj:
                         t.matmul(pt[:, P * (c % 4): P * (c % 4 + 1)],
                                  pq[:, 2048 * hb + 256 * c + P * jj: 2048 * hb + 256 * c + P * (jj + 1)],
                                  kvb[:, P * c: P * (c + 1)], start=True, stop=True),
                         waits=tuple(aw) if c == 0 else ())
                    s.op("tensor", lambda t, c=c, hb=hb, jj=jj:
                         t.matmul(PDEN[:, :],
                                  pq[:, 2048 * hb + 256 * c + P * jj: 2048 * hb + 256 * c + P * (jj + 1)],
                                  ksb[:, H * c: H * (c + 1)],
                                  start=(c == 0), stop=(c == EC - 1)),
                         wp=(f"t_attn{j}" if c == EC - 1 else None))

                s.op("vector", lambda v: v.tensor_scalar(zbuf[:, 0:H], PDEN[:, :], 1e-6, None, ALU.add),
                     waits=(f"t_attn{j}", f"ld_x_{tag}"))
                s.op("vector", lambda v: v.reciprocal(zbuf[:, H:2 * H], zbuf[:, 0:H]), wp=f"v_z_{j}")
                for h in range(H):
                    pt = P3 if h < 8 else P4
                    s.op("vector", lambda v, h=h, pt=pt, slot=slot: v.scalar_tensor_tensor(
                        xa[:, E * slot + D * h: E * slot + D * (h + 1)],
                        pt[:, D * (h % 8): D * (h % 8 + 1)],
                        zbuf[:, H + h: H + h + 1],
                        xa[:, E * slot + D * h: E * slot + D * (h + 1)],
                        ALU.mult, ALU.add),
                        waits=(f"v_z_{j}",) if h == 0 else (),
                        wp=f"v_az_{j}" if h == H - 1 else None,
                        nodrain=(h >= 1))

        def emit_ln2(bg):
            for jj in range(BG):
                j = BG * bg + jj
                tag = f"b{j}"
                if j >= 2:
                    lg = f"t_tr_b{j - 2}"
                elif jj == 0:
                    lg = f"t_tr_a{NT - 1}"
                else:
                    lg = f"t_kvmm{NT - 1}"
                emit_ln(tag, j % 4, (f"v_az_{j}",), lg, (),
                        buf=(lntok if jj == 0 else pk), par=j % 2)

        def emit_trans2(bg):
            hb = bg % 2
            for jj in range(BG):
                emit_trans(f"b{BG * bg + jj}", dst_off=2048 * hb + P * jj, dst_step=BG * P,
                           lnf_gate=(f"t_f1{bg - 2}_{FC - 1}" if bg >= 2 else None),
                           buf=(lntok if jj == 0 else pk))

        def emit_ffn1(bg):
            hb = bg % 2
            for f in range(FC):
                pt = KV1 if f % 2 == 0 else KV2
                wts = [f"v_lnf_b{BG * bg + BG - 1}"] + [f"ld_w1{e}" for e in range(EC)]
                if bg == 0 and f < 2:
                    wts.append(f"v_kvadd{NT - 1}")
                if f >= 2:
                    wts.append(f"s_gel{bg}_{f - 2}")
                elif bg >= 1:
                    wts.append(f"s_gel{bg - 1}_{FC - 2 + f}")
                for e in range(EC):
                    s.op("tensor", lambda t, e=e, f=f, pt=pt, hb=hb:
                         t.matmul(pt[:, 0:256],
                                  WREG[:, W1_OFF + FF * e + P * f: W1_OFF + FF * e + P * (f + 1)],
                                  lnf[:, 2048 * hb + BG * P * e: 2048 * hb + BG * P * (e + 1)],
                                  start=(e == 0), stop=(e == EC - 1)),
                         waits=tuple(wts) if e == 0 else (),
                         wp=f"t_f1{bg}_{f}" if e == EC - 1 else None)
                gw = [f"t_f1{bg}_{f}", "v_b1pp"]
                if bg >= 1 and f == 0:
                    gw.append(f"t_f2{bg - 1}_{BG - 1}")
                s.op("scalar", lambda sc, f=f, pt=pt: sc.activation(
                    hfm[:, 256 * f: 256 * (f + 1)], pt[:, 0:256], AF.Gelu, bias=b1pp[:, f:f + 1]),
                    waits=tuple(gw), wp=f"s_gel{bg}_{f}")

        def emit_ffn2(bg):
            for jj in range(BG):
                j = BG * bg + jj
                wts = [f"s_gel{bg}_{FC - 1}", f"s_ln_b{j}"]
                if bg == 0:
                    wts += ["ld_w2h0", "ld_w2h1"]
                if jj == 0:
                    if bg >= 1:
                        wts += [f"f2evac{bg - 1}_1_0", f"f2evac{bg - 1}_1_1"]
                    else:
                        wts += [f"v_phi_q{NAG - 1}_{EC - 2}", f"v_phi_q{NAG - 1}_{EC - 1}",
                                f"s_ex_q{NAG - 1}_{EC - 2}", f"s_ex_q{NAG - 1}_{EC - 1}"]
                else:
                    wts += [f"f2evac{bg}_0_0", f"f2evac{bg}_0_1"]
                for f in range(FC):
                    fw = list(wts) if f == 0 else []
                    if bg == 0 and f >= 8 and (f - 8) % 4 == 0:
                        fw.append(f"ld_w2t{(f - 8) // 4}")
                    s.op("tensor", lambda t, f=f, jj=jj:
                         t.matmul(PA[:, :], hfm[:, 256 * f + P * jj: 256 * f + P * (jj + 1)],
                                  WREG[:, W2_OFF + E * f: W2_OFF + E * f + 512],
                                  start=(f == 0), stop=False),
                         waits=tuple(fw))
                    s.op("tensor", lambda t, f=f, jj=jj:
                         t.matmul(PB[:, :], hfm[:, 256 * f + P * jj: 256 * f + P * (jj + 1)],
                                  WREG[:, W2_OFF + E * f + 512: W2_OFF + E * f + 1024],
                                  start=(f == 0), stop=False))
                s.op("tensor", lambda t:
                     t.matmul(PA[:, :], brows[0:1, 4 * E: 4 * E + P],
                              brows[0:1, 3 * E: 3 * E + 512], start=False, stop=True))
                s.op("tensor", lambda t:
                     t.matmul(PB[:, :], brows[0:1, 4 * E: 4 * E + P],
                              brows[0:1, 3 * E + 512: 4 * E], start=False, stop=True),
                     wp=f"t_f2{bg}_{jj}")
                for ec in range(2):
                    def _f2ev(v, slot=j % 4, ec=ec):
                        xsl = xa[:, E * slot + 512 * ec: E * slot + 512 * (ec + 1)]
                        return v.tensor_tensor(xsl, xsl, (PA if ec == 0 else PB)[:, :], ALU.add)
                    s.op("vector", _f2ev, waits=(f"t_f2{bg}_{jj}",) if ec == 0 else (),
                         wp=f"f2evac{bg}_{jj}_{ec}", nodrain=(ec == 1))
                slot = j % 4
                dma("scalar", out_h[P * j: P * (j + 1), :], xa[:, E * slot: E * (slot + 1)],
                    waits=(f"f2evac{bg}_{jj}_0", f"f2evac{bg}_{jj}_1"), wp=f"sp_out{j}")

        for bg in range(NBG):
            emit_attn(bg)
            if bg >= 1:
                emit_ffn1(bg - 1)
            emit_ln2(bg)
            if bg >= 1:
                emit_ffn2(bg - 1)
            emit_trans2(bg)
        emit_ffn1(NBG - 1)
        emit_ffn2(NBG - 1)

        s.op("scalar", lambda sc: sc.nop(), waits=(f"sp_out{NT - 1}",))

        @block.sync
        def _(e):
            s.emit("sync", e, sems)

        @block.gpsimd
        def _(e):
            s.emit("gpsimd", e, sems)

        @block.tensor
        def _(e):
            s.emit("tensor", e, sems)

        @block.vector
        def _(e):
            s.emit("vector", e, sems)

        @block.scalar
        def _(e):
            s.emit("scalar", e, sems)

    return nc


def kernel(L0_attn_x, L0_attn_W_qkv, L0_attn_b_qkv, L0_attn_W_proj, L0_attn_b_proj,
           L0_ffn_w1, L0_ffn_b1, L0_ffn_w2, L0_ffn_b2):
    x = np.ascontiguousarray(np.asarray(L0_attn_x, dtype=np.float32))
    wqkv = np.ascontiguousarray(np.asarray(L0_attn_W_qkv, dtype=np.float32))
    bqkv = np.ascontiguousarray(np.asarray(L0_attn_b_qkv, dtype=np.float32))
    w1 = np.ascontiguousarray(np.asarray(L0_ffn_w1, dtype=np.float32))
    b1 = np.ascontiguousarray(np.asarray(L0_ffn_b1, dtype=np.float32))
    w2 = np.ascontiguousarray(np.asarray(L0_ffn_w2, dtype=np.float32))
    b2 = np.ascontiguousarray(np.asarray(L0_ffn_b2, dtype=np.float32))

    nc = build_nc()
    in_maps = []
    for core in range(8):
        b = core // 4
        t0 = TOK * (core % 4)
        bm = np.zeros((128, 2), dtype=np.float32)
        bm[:, b] = 1.0
        in_maps.append({
            "x": np.ascontiguousarray(x[b, t0:t0 + TOK, :]),
            "wqkv": wqkv, "bqkv": bqkv, "w1": w1, "b1": b1, "w2": w2, "b2": b2,
            "bmask": bm,
        })
    res = run_bass_kernel_spmd(nc, in_maps, core_ids=list(range(8)))
    out = np.empty((2, 8192, 1024), dtype=np.float32)
    for core in range(8):
        b = core // 4
        t0 = TOK * (core % 4)
        out[b, t0:t0 + TOK, :] = res.results[core]["out"]
    return out
